# version 04
# baseline (speedup 1.0000x reference)
import math
import numpy as np
import ml_dtypes
from contextlib import ExitStack
import concourse.bass as bass
import concourse.mybir as mybir
from concourse.bass_utils import run_bass_kernel_spmd

F32 = mybir.dt.float32
BF16 = mybir.dt.bfloat16
AF = mybir.ActivationFunctionType
ALU = mybir.AluOpType
AX = mybir.AxisListType
BF = ml_dtypes.bfloat16

N = 2048
D = 2048
DC = 16
I8 = 8192
IC = 64
HEADS = 16
G1 = 4
TSB = 4
TWO_PI = 6.283185307179586
PI = 3.141592653589793
LN1E4 = 9.210340371976184
C2048 = 1.0 / math.sqrt(2048.0)
C8192 = 1.0 / math.sqrt(8192.0)
QSC = 1.0 / math.sqrt(128.0)

ENGS = ("tensor", "vector", "scalar", "sync")
PHASES = 99


class Seq:

    def __init__(self):
        self.items = []
        self.counts = {e: 0 for e in ENGS}
        self.seen = {e: {o: 0 for o in ENGS} for e in ENGS}
        self.dma_counts = {}
        self.dma_seen = {e: {} for e in ENGS}

    def dma_tick(self, fam):
        self.dma_counts[fam] = self.dma_counts.get(fam, 0) + 16
        return self.dma_counts[fam]

    def step(self, eng, fn, dma_waits=(), auto=True, inc=True, ewaits=()):
        waits = []
        if auto:
            for o in ENGS:
                if o != eng and self.counts[o] > self.seen[eng][o]:
                    waits.append((o, self.counts[o]))
                    self.seen[eng][o] = self.counts[o]
        for (o, cnt) in ewaits:
            if cnt > self.seen[eng][o]:
                waits.append((o, cnt))
                self.seen[eng][o] = cnt
        dw = []
        for (f, cnt) in dma_waits:
            if cnt > self.dma_seen[eng].get(f, 0):
                dw.append((f, cnt))
                self.dma_seen[eng][f] = cnt
        self.items.append((eng, waits, dw, fn, inc))
        if inc:
            self.counts[eng] += 1
        return self.counts[eng]


class Prog:
    def __init__(self, phases=99):
        self.phases = phases
        self.nc = bass.Bass("TRN2", target_bir_lowering=False)
        self.seq = Seq()
        self.es = ExitStack()
        self._declare()

    def _declare(self):
        nc = self.nc
        din = lambda n_, sh, dt=F32: nc.dram_tensor(n_, sh, dt, kind="ExternalInput").ap()
        self.inp = {}
        for n_, sh, dt in [
            ("x_t", [D, N], F32),
            ("l0_wq", [D, D], F32), ("l0_wk", [D, D], F32),
            ("l0_wv", [D, D], F32), ("l0_wo", [D, D], F32),
            ("l0_wg", [D, I8], F32), ("l0_wu", [D, I8], F32),
            ("l0_wd", [I8, D], F32),
            ("l1_wq", [D, D], F32), ("l1_wk", [D, G1 * 128], F32),
            ("l1_wv", [D, G1 * 128], F32), ("l1_wo", [D, D], F32),
            ("l1_agg", [D, D], F32), ("l1_bc", [D, D], F32),
            ("l1_gw", [D, D], F32),
            ("l1_d1", [D, I8], F32), ("l1_d2", [I8, D], F32),
            ("l1_d1b", [1, I8], F32), ("l1_d2b", [1, D], F32),
            ("gam", [128, 7 * DC], F32),
            ("aggb_c", [128, DC], F32), ("bcb_c", [128, DC], F32),
            ("c_ones", [128, 128], BF16), ("c_iden", [128, 128], BF16),
            ("c_had", [128, 128], BF16), ("c_mk", [128, 896], BF16),
            ("c_pos", [1, N], F32), ("c_j64", [128, 1], F32),
            ("c_neg1", [1, 512], F32), ("c_onesf", [128, 1], F32),
            ("c_onerf", [1, 128], F32),
        ]:
            self.inp[n_] = din(n_, sh, dt)
        self.y_t = nc.dram_tensor("y_t", [D, N], F32, kind="ExternalOutput").ap()
        self.trunc = self.phases < 99
        if self.trunc:
            self.dbg = nc.dram_tensor("dbg", [128, 2 * DC * N + 4608], BF16,
                                      kind="ExternalOutput").ap()
        self.s_rc = nc.dram_tensor("s_rc", [128, 4], F32, kind="Internal").ap()
        self.s_rr = nc.dram_tensor("s_rr", [1, 512], BF16, kind="Internal").ap()
        self.s_rf = nc.dram_tensor("s_rf", [1, 512], F32, kind="Internal").ap()
        self.s_h = nc.dram_tensor("s_h", [I8, N], BF16,
                                  kind="ExternalOutput" if self.trunc
                                  else "Internal").ap()
        self.s_wd = nc.dram_tensor("s_wd", [I8, D], BF16, kind="Internal").ap()
        self.s_d1 = nc.dram_tensor("s_d1", [D, I8], BF16, kind="Internal").ap()
        self.s_d2 = nc.dram_tensor("s_d2", [I8, D], BF16, kind="Internal").ap()

        es = self.es
        sb = lambda n_, sh, dt=BF16: es.enter_context(nc.sbuf_tensor(n_, sh, dt))
        ps = lambda n_, sh, dt=F32: es.enter_context(nc.psum_tensor(n_, sh, dt))
        self.X = sb("X", [128, DC * N])
        self.A = sb("A", [128, DC * N])
        self.HB = sb("HB", [128, 4096])
        self.B = sb("B", [128, 2048])
        self.C = sb("C", [128, 4096])
        self.WT = sb("WT", [128, 2048])
        self.W32 = sb("W32", [128, 512], F32)
        self.VF = sb("VF", [128, 2048], F32)
        self.MSW = sb("MSW", [128, 1024], F32)
        self.RBC = sb("RBC", [128, 2048])
        self.ROWF = sb("ROWF", [128, 1024], F32)
        self.RB = sb("RB", [128, 4608])
        self.SC = sb("SC", [128, 64], F32)
        self.SCB = sb("SCB", [128, 96])
        self.THC = sb("THC", [128, DC], F32)
        self.BCC = sb("BCC", [128, DC], F32)
        self.ACC = sb("ACC", [128, 8], F32)
        self.ZT = sb("ZT", [128, 128])
        self.Z512 = sb("Z512", [128, 512], F32)
        self.CONES = sb("CONES", [128, 128])
        self.CIDEN = sb("CIDEN", [128, 128])
        self.CHAD = sb("CHAD", [128, 128])
        self.CMK = sb("CMK", [128, 896])
        self.POS = sb("POS", [128, N], F32)
        self.J64 = sb("J64", [128, 1], F32)
        self.NEG1 = sb("NEG1", [128, 512], F32)
        self.ONE1F = sb("ONE1F", [128, 1], F32)
        self.ONERF = sb("ONERF", [128, 128], F32)
        self.GAM = sb("GAM", [128, 7 * DC], F32)
        self.AGGBC = sb("AGGBC", [128, DC], F32)
        self.BCBC = sb("BCBC", [128, DC], F32)
        self.EPSC = sb("EPSC", [128, 2], F32)

        self.P0 = ps("P0", [128, 512])
        self.P1 = ps("P1", [128, 512])
        self.P2 = ps("P2", [128, 512])
        self.P3 = ps("P3", [128, 512])
        self.P4 = ps("P4", [128, 512])
        self.P5 = ps("P5", [128, 512])
        self.PR = ps("PR", [128, 512])
        self.PTB = ps("PTB", [128, 512], BF16)
        self.PB = {0: self.P0, 1: self.P2, 2: self.P3, 3: self.P5}

    def xch(self, ch):
        return self.X[:, ch * N:(ch + 1) * N]

    def ach(self, ch):
        return self.A[:, ch * N:(ch + 1) * N]

    def aslot512(self, s):
        return self.A[:, s * 512:(s + 1) * 512]

    def vf(self, i):
        return self.VF[:, i * 512:(i + 1) * 512]

    def bsl(self, i):
        return self.B[:, i * 512:(i + 1) * 512]

    def pe(self, fn, **kw):
        self.seq.step("tensor", fn, **kw)

    def ve(self, fn, **kw):
        self.seq.step("vector", fn, **kw)

    def se(self, fn, **kw):
        self.seq.step("scalar", fn, **kw)

    def load(self, fam, dst, src, ewaits=None, dwaits=(), slow=False):
        tgt = self.seq.dma_tick(fam)
        def fn(h, s, dst=dst, src=src, fam=fam, slow=slow):
            return h.dma_start(out=dst, in_=src,
                               allow_slow_non_contiguous=slow
                               ).then_inc(s.dsem[fam], 16)
        if ewaits is None:
            self.seq.step("sync", fn, auto=True, inc=False, dma_waits=dwaits)
        else:
            self.seq.step("sync", fn, auto=False, inc=False, ewaits=ewaits,
                          dma_waits=dwaits)
        return tgt

    def col_to_row(self, src_cols, dst_row):
        prc = self.seq.dma_counts.get("rcr", 0)
        t1 = self.store("rcw", self.s_rc, src_cols,
                        dwaits=[("rcr", prc)] if prc else ())
        return self.load("rcr", dst_row.rearrange("a (c o) -> a c o", o=128),
                         self.s_rc.rearrange("p c -> () c p"),
                         dwaits=[("rcw", t1)], slow=True)

    def store(self, fam, dst, src, dwaits=()):
        tgt = self.seq.dma_tick(fam)
        def fn(h, s, dst=dst, src=src, fam=fam):
            return h.dma_start(out=dst, in_=src).then_inc(s.dsem[fam], 16)
        self.seq.step("sync", fn, auto=True, inc=False, dma_waits=dwaits)
        return tgt

    def mm(self, out, lhsT, rhs, start=True, stop=True):
        def fn(h, s, out=out, lhsT=lhsT, rhs=rhs, a=start, b=stop):
            return h.matmul(out, lhsT, rhs, start=a, stop=b)
        self.pe(fn)

    def ph_init(self):
        sq = self.seq
        for dst, src in [(self.CONES[:], self.inp["c_ones"]),
                         (self.CIDEN[:], self.inp["c_iden"]),
                         (self.CHAD[:], self.inp["c_had"]),
                         (self.CMK[:], self.inp["c_mk"]),
                         (self.POS[0:1, :], self.inp["c_pos"]),
                         (self.J64[:], self.inp["c_j64"]),
                         (self.NEG1[0:1, :], self.inp["c_neg1"]),
                         (self.ONE1F[:], self.inp["c_onesf"]),
                         (self.ONERF[0:1, :], self.inp["c_onerf"]),
                         (self.GAM[:], self.inp["gam"]),
                         (self.AGGBC[:], self.inp["aggb_c"]),
                         (self.BCBC[:], self.inp["bcb_c"])]:
            self.load("cst", dst, src)
        cst = sq.dma_counts["cst"]
        def f0(h, s):
            h.memset(self.ZT[:], 0.0)
            h.memset(self.Z512[:], 0.0)
            h.memset(self.EPSC[:, 0:1], 1e-5)
            return h.memset(self.EPSC[:, 1:2], 1e-6)
        self.ve(f0, dma_waits=[("cst", cst)])
        for ch in range(DC):
            for j in range(TSB):
                tgt = self.load("xin", self.W32[:, 0:512],
                                self.inp["x_t"][ch * 128:(ch + 1) * 128,
                                                j * 512:(j + 1) * 512],
                                ewaits=[("vector", sq.counts["vector"])])
                def fc(h, s, ch=ch, j=j):
                    return h.tensor_copy(self.xch(ch)[:, j * 512:(j + 1) * 512],
                                         self.W32[:, 0:512])
                self.ve(fc, dma_waits=[("xin", tgt)])
        def fj1(h, s):
            return h.tensor_scalar(self.ROWF[0:1, 128:256], self.POS[0:1, 0:128],
                                   64.0, 64.0, ALU.is_ge, ALU.mult)
        self.ve(fj1)
        def fj2(h, s):
            return h.tensor_tensor(self.ROWF[0:1, 256:384], self.POS[0:1, 0:128],
                                   self.ROWF[0:1, 128:256], ALU.subtract)
        self.ve(fj2)
        def fth(h, s):
            return h.activation(self.ROWF[0:1, 0:128], self.ROWF[0:1, 256:384],
                                AF.Exp, scale=-LN1E4 / 64.0)
        self.se(fth)
        def fsg1(h, s):
            return h.tensor_scalar(self.ROWF[0:1, 128:256], self.POS[0:1, 0:128],
                                   64.0, 2.0, ALU.is_ge, ALU.mult)
        self.ve(fsg1)
        def fsg2(h, s):
            return h.tensor_scalar(self.ROWF[0:1, 256:384],
                                   self.ROWF[0:1, 128:256], -1.0, None, ALU.add)
        self.ve(fsg2)
        def fsg3(h, s):
            return h.tensor_tensor(self.ROWF[0:1, 0:128], self.ROWF[0:1, 0:128],
                                   self.ROWF[0:1, 256:384], ALU.mult)
        self.ve(fsg3)
        for j in range(TSB):
            tsl = slice(j * 512, (j + 1) * 512)
            self.mm(self.P4[:, 0:512], self.ROWF[0:1, 0:128], self.POS[0:1, tsl])
            for trig, off in ((0, 0.0), (1, PI / 2)):
                def fu(h, s, off=off):
                    return h.tensor_scalar(self.vf(0), self.P4[:, 0:512],
                                           1.0 / TWO_PI,
                                           512.5 + off / TWO_PI,
                                           ALU.mult, ALU.add)
                self.ve(fu)
                def fu2(h, s):
                    return h.tensor_copy(self.vf(1), self.vf(0))
                self.ve(fu2)
                dcur, dnxt = 1, 2
                for k in range(9, -1, -1):
                    def fb(h, s, k=k, dc=dcur):
                        return h.tensor_scalar(self.vf(3), self.vf(dc),
                                               float(2 ** k), float(2 ** k),
                                               ALU.is_ge, ALU.mult)
                    self.ve(fb)
                    def fd(h, s, dc=dcur, dn=dnxt):
                        return h.tensor_tensor(self.vf(dn), self.vf(dc),
                                               self.vf(3), ALU.subtract)
                    self.ve(fd)
                    dcur, dnxt = dnxt, dcur
                def fk(h, s, dc=dcur):
                    return h.tensor_tensor(self.vf(3), self.vf(0), self.vf(dc),
                                           ALU.subtract)
                self.ve(fk)
                def fr(h, s):
                    return h.scalar_tensor_tensor(self.vf(1), self.vf(3),
                                                  -TWO_PI, self.P4[:, 0:512],
                                                  ALU.mult, ALU.add)
                self.ve(fr)
                def fr2(h, s, off=off):
                    return h.tensor_scalar(self.vf(2), self.vf(1),
                                           off + 1024.0 * PI, -PI,
                                           ALU.add, ALU.max)
                self.ve(fr2)
                def fr3(h, s):
                    return h.tensor_scalar(self.vf(3), self.vf(2), PI, None,
                                           ALU.min)
                self.ve(fr3)
                def fs(h, s, trig=trig, tsl=tsl):
                    return h.activation(
                        self.C[:, trig * N + tsl.start: trig * N + tsl.stop],
                        self.vf(3), AF.Sin)
                self.se(fs)

    def ph_stats(self, eps_col, src=None):
        src = src or self.xch
        for j in range(TSB):
            tsl = slice(j * 512, (j + 1) * 512)
            def fsum(h, s, tsl=tsl, src=src):
                for ch in range(DC):
                    r = h.matmul(self.PR[0:1, 0:512], self.CONES[:, 0:1],
                                 src(ch)[:, tsl], start=(ch == 0),
                                 stop=(ch == DC - 1))
                return r
            self.pe(fsum)
            for ch in range(DC):
                def fsq(h, s, ch=ch, tsl=tsl, src=src):
                    return h.activation(self.bsl(3), src(ch)[:, tsl],
                                        AF.Square)
                self.se(fsq)
                def fss(h, s, ch=ch):
                    return h.matmul(self.PR[32:33, 0:512], self.CONES[:, 0:1],
                                    self.bsl(3), start=(ch == 0),
                                    stop=(ch == DC - 1))
                self.pe(fss)
            def fnm(h, s, tsl=tsl):
                return h.tensor_scalar(self.RB[0:1, tsl], self.PR[0:1, 0:512],
                                       -1.0 / D, None, ALU.mult)
            self.ve(fnm)
            def fmu(h, s):
                return h.tensor_scalar(self.ROWF[0:1, 0:512],
                                       self.PR[0:1, 0:512], 1.0 / D, None,
                                       ALU.mult)
            self.ve(fmu)
            def fm2(h, s):
                return h.tensor_tensor(self.ROWF[0:1, 512:1024],
                                       self.ROWF[0:1, 0:512],
                                       self.ROWF[0:1, 0:512], ALU.mult)
            self.ve(fm2)
            def fvar(h, s):
                return h.scalar_tensor_tensor(self.ROWF[0:1, 0:512],
                                              self.PR[32:33, 0:512], 1.0 / D,
                                              self.ROWF[0:1, 512:1024],
                                              ALU.mult, ALU.subtract)
            self.ve(fvar)
            epsv = 1e-5 if eps_col == 0 else 1e-6
            def fre(h, s, epsv=epsv):
                return h.tensor_scalar(self.ROWF[0:1, 512:1024],
                                       self.ROWF[0:1, 0:512], epsv, None,
                                       ALU.add)
            self.ve(fre)
            def frr(h, s):
                return h.reciprocal(self.ROWF[0:1, 0:512],
                                    self.ROWF[0:1, 512:1024])
            self.ve(frr)
            def fr(h, s, tsl=tsl):
                return h.activation(self.RB[0:1, N + tsl.start:N + tsl.stop],
                                    self.ROWF[0:1, 0:512], AF.Sqrt)
            self.se(fr)
            def fbc(h, s, tsl=tsl):
                return h.matmul(self.P4[:, 0:512], self.CONES[0:1, :],
                                self.RB[0:1, N + tsl.start:N + tsl.stop],
                                start=True, stop=True)
            self.pe(fbc)
            def fcp(h, s, tsl=tsl):
                return h.activation(self.RBC[:, tsl], self.P4[:, 0:512], AF.Copy)
            self.se(fcp)

    def load_wtile(self, fam, src, c0, scale=None, wsum=False, nch=DC):
        sq = self.seq
        rearr = src[:, c0:c0 + 128].rearrange("(c p) o -> p c o", p=128)
        dst3 = self.W32[:, 0:512].rearrange("p (c o) -> p c o", o=128)
        for q in range(nch // 4):
            sl = rearr[:, q * 4:(q + 1) * 4, :]
            tgt = self.load(fam, dst3, sl,
                            ewaits=[("vector", sq.counts["vector"])])
            if scale is None:
                def fc(h, s, q=q):
                    return h.tensor_copy(self.WT[:, q * 512:(q + 1) * 512],
                                         self.W32[:, 0:512])
            else:
                def fc(h, s, q=q, scale=scale):
                    return h.tensor_scalar(self.WT[:, q * 512:(q + 1) * 512],
                                           self.W32[:, 0:512], scale, None,
                                           ALU.mult)
            self.ve(fc, dma_waits=[(fam, tgt)])
        if wsum:
            def fws(h, s, nch=nch):
                for ch in range(nch):
                    r = h.matmul(self.PR[64:65, 0:128], self.CONES[:, 0:1],
                                 self.WT[:, ch * 128:(ch + 1) * 128],
                                 start=(ch == 0), stop=(ch == nch - 1))
                return r
            self.pe(fws)
            def fwc(h, s):
                return h.tensor_copy(self.RB[0:1, 2 * N:2 * N + 128],
                                     self.PR[64:65, 0:128])
            self.ve(fwc)

    def proj_psum(self, pbank, tsl, nch=DC, fold=True, src=None):
        src = src or self.xch
        def fp(h, s, tsl=tsl, nch=nch, fold=fold, pbank=pbank, src=src):
            for ch in range(nch):
                r = h.matmul(pbank[:, 0:512], self.WT[:, ch * 128:(ch + 1) * 128],
                             src(ch)[:, tsl], start=(ch == 0),
                             stop=(ch == nch - 1 and not fold))
            if fold:
                r = h.matmul(pbank[:, 0:512], self.RB[0:1, 2 * N:2 * N + 128],
                             self.RB[0:1, tsl], start=False, stop=True)
            return r
        self.pe(fp)

    def evac_rope(self, dst, tsl):
        def f1(h, s, tsl=tsl):
            return h.tensor_tensor(self.vf(0), self.P0[:, 0:512],
                                   self.RBC[:, tsl], ALU.mult)
        self.ve(f1)
        def f2(h, s):
            return h.tensor_copy(self.vf(1)[0:64, :], self.vf(0)[64:128, :])
        self.ve(f2)
        def f3(h, s):
            return h.tensor_copy(self.vf(1)[64:128, :], self.vf(0)[0:64, :])
        self.ve(f3)
        def f4(h, s, tsl=tsl):
            return h.tensor_tensor(self.vf(2), self.vf(0),
                                   self.C[:, tsl.start + N:tsl.stop + N],
                                   ALU.mult)
        self.ve(f4)
        def f5(h, s, tsl=tsl):
            return h.tensor_tensor(self.vf(3), self.vf(1),
                                   self.C[:, tsl.start:tsl.stop], ALU.mult)
        self.ve(f5)
        def f6(h, s, dst=dst):
            return h.tensor_tensor(dst, self.vf(2), self.vf(3), ALU.add)
        self.ve(f6)

    def ph_l0_attn(self):
        for hd in range(HEADS):
            self.load_wtile("wp", self.inp["l0_wk"], hd * 128, wsum=True)
            for j in range(TSB):
                tsl = slice(j * 512, (j + 1) * 512)
                self.proj_psum(self.P0, tsl)
                self.evac_rope(self.HB[:, tsl], tsl)
            self.load_wtile("wp", self.inp["l0_wv"], hd * 128, wsum=True)
            for j in range(TSB):
                tsl = slice(j * 512, (j + 1) * 512)
                self.proj_psum(self.P0, tsl)
                def fv(h, s, tsl=tsl):
                    return h.tensor_tensor(self.bsl(3), self.P0[:, 0:512],
                                           self.RBC[:, tsl], ALU.mult)
                self.ve(fv)
                def ftr(h, s):
                    for i in range(4):
                        r = h.transpose(self.PTB[:, i * 128:(i + 1) * 128],
                                        self.B[:, 1536 + i * 128:1536 + (i + 1) * 128],
                                        self.CIDEN[:])
                    return r
                self.pe(ftr)
                def fvc(h, s, j=j):
                    return h.activation(self.HB[:, 2048 + j * 512:2048 + (j + 1) * 512],
                                        self.PTB[:, 0:512], AF.Copy)
                self.se(fvc)
            self.load_wtile("wp", self.inp["l0_wq"], hd * 128, wsum=True)
            for qb in range(TSB):
                qsl = slice(qb * 512, (qb + 1) * 512)
                self.proj_psum(self.P0, qsl)
                self.evac_rope(self.bsl(0), qsl)
                nkt = (qb + 1) * 4
                for kt in range(nkt):
                    uu = kt - 4 * qb
                    def fsc(h, s, kt=kt):
                        return h.matmul(self.P2[:, 0:512],
                                        self.HB[:, kt * 128:(kt + 1) * 128],
                                        self.bsl(0), start=True, stop=True)
                    self.pe(fsc)
                    def fs2(h, s):
                        return h.activation(self.bsl(1), self.P2[:, 0:512],
                                            AF.Square)
                    self.se(fs2)
                    def fs4(h, s):
                        return h.activation(self.bsl(2), self.bsl(1), AF.Square)
                    self.se(fs4)
                    cur = 2
                    if uu >= 0:
                        mo = 384 - 128 * uu
                        def fmk(h, s, mo=mo):
                            return h.tensor_tensor(self.bsl(1), self.bsl(2),
                                                   self.CMK[:, mo:mo + 512],
                                                   ALU.mult)
                        self.ve(fmk)
                        cur = 1
                    def fden(h, s, cur=cur, kt=kt, nkt=nkt):
                        return h.matmul(self.PR[0:1, 0:512], self.CONES[:, 0:1],
                                        self.bsl(cur), start=(kt == 0),
                                        stop=(kt == nkt - 1))
                    self.pe(fden)
                    def fpv(h, s, cur=cur, kt=kt, nkt=nkt):
                        return h.matmul(self.P3[:, 0:512],
                                        self.HB[:, 2048 + kt * 128:2048 + (kt + 1) * 128],
                                        self.bsl(cur), start=(kt == 0),
                                        stop=(kt == nkt - 1))
                    self.pe(fpv)
                def fdn(h, s):
                    return h.tensor_scalar(self.ROWF[0:1, 0:512],
                                           self.PR[0:1, 0:512], 1e-6, None,
                                           ALU.max)
                self.ve(fdn)
                def frc(h, s):
                    return h.reciprocal(self.ROWF[0:1, 512:1024],
                                        self.ROWF[0:1, 0:512])
                self.ve(frc)
                def fb(h, s):
                    return h.matmul(self.P4[:, 0:512], self.ONERF[0:1, :],
                                    self.ROWF[0:1, 512:1024], start=True,
                                    stop=True)
                self.pe(fb)
                def fcp(h, s):
                    return h.activation(self.vf(0), self.P4[:, 0:512], AF.Copy)
                self.se(fcp)
                def fo(h, s, hd=hd, qsl=qsl):
                    return h.tensor_tensor(self.ach(hd)[:, qsl],
                                           self.P3[:, 0:512], self.vf(0),
                                           ALU.mult)
                self.ve(fo)
        self._wo_residual(self.inp["l0_wo"])

    def _wo_residual(self, wsrc):
        for oc in range(DC):
            self.load_wtile("wo", wsrc, oc * 128)
            for j in range(TSB):
                tsl = slice(j * 512, (j + 1) * 512)
                self.proj_psum(self.P0, tsl, fold=False, src=self.ach)
                def fa(h, s, oc=oc, tsl=tsl):
                    return h.tensor_tensor(self.xch(oc)[:, tsl],
                                           self.xch(oc)[:, tsl],
                                           self.P0[:, 0:512], ALU.add)
                self.ve(fa)

    def ph_prepass(self):
        sq = self.seq
        specs = [("wg", self.inp["l0_wg"], D, I8, 0),
                 ("wu", self.inp["l0_wu"], D, I8, 2),
                 ("wd", self.inp["l0_wd"], I8, D, 4)]
        for nm, src, R, Cc, sci in specs:
            def fz(h, s, sci=sci):
                return h.memset(self.ACC[:, sci:sci + 1], 0.0)
            self.ve(fz)
            ntile = (R // 128) * (Cc // 512)
            ti = 0
            for r in range(R // 128):
                for cc in range(0, Cc, 512):
                    tgt = self.load("ffA", self.W32[:, 0:512],
                                    src[r * 128:(r + 1) * 128, cc:cc + 512],
                                    ewaits=[("scalar", sq.counts["scalar"]),
                                            ("vector", sq.counts["vector"])])
                    def fab(h, s, sci=sci):
                        return h.activation(self.vf(0), self.W32[:, 0:512],
                                            AF.Abs,
                                            accum_out=self.ACC[:, sci + 1:sci + 2])
                    self.se(fab, dma_waits=[("ffA", tgt)])
                    def fac(h, s, sci=sci):
                        return h.tensor_tensor(self.ACC[:, sci:sci + 1],
                                               self.ACC[:, sci:sci + 1],
                                               self.ACC[:, sci + 1:sci + 2],
                                               ALU.add)
                    self.ve(fac)
                    ti += 1
            def ft(h, s, sci=sci):
                return h.matmul(self.PR[0:1, 0:1], self.ONE1F[:],
                                self.ACC[:, sci:sci + 1], start=True, stop=True)
            self.pe(ft)
            def fsb(h, s, RC=float(R) * float(Cc)):
                return h.tensor_scalar(self.ROWF[0:1, 0:1], self.PR[0:1, 0:1],
                                       1.0 / RC, 1e-5, ALU.mult, ALU.max)
            self.ve(fsb)
            def fb1(h, s):
                return h.matmul(self.P4[:, 0:1], self.ONERF[0:1, :],
                                self.ROWF[0:1, 0:1], start=True, stop=True)
            self.pe(fb1)
            def fb2(h, s, sci=sci):
                return h.activation(self.SC[:, sci:sci + 1], self.P4[:, 0:1],
                                    AF.Copy)
            self.se(fb2)
            def fiv(h, s, sci=sci):
                return h.reciprocal(self.SC[:, sci + 1:sci + 2],
                                    self.SC[:, sci:sci + 1])
            self.ve(fiv)
        sslot = {}
        si = 0
        for r in range(I8 // 128):
            for cc in range(0, D, 512):
                tgt = self.load("ffB", self.W32[:, 0:512],
                                self.inp["l0_wd"][r * 128:(r + 1) * 128,
                                                  cc:cc + 512],
                                ewaits=[("vector", sq.counts["vector"])])
                bo = 512 * (si % 2)
                dw = [("ffB", tgt)]
                if (si % 2) in sslot:
                    dw.append(("scr", sslot[si % 2]))
                def fq(h, s, bo=bo):
                    h.scalar_tensor_tensor(self.vf(0), self.W32[:, 0:512],
                                           self.SC[:, 5:6], self.Z512[:],
                                           ALU.mult, ALU.add)
                    h.tensor_scalar(self.vf(1), self.vf(0), 0.5, 1.0,
                                    ALU.is_ge, ALU.mult)
                    h.tensor_scalar(self.vf(2), self.vf(0), -0.5, 1.0,
                                    ALU.is_lt, ALU.mult)
                    return h.tensor_tensor(self.B[:, bo:bo + 512], self.vf(1),
                                           self.vf(2), ALU.subtract)
                self.ve(fq, dma_waits=dw)
                sslot[si % 2] = self.store(
                    "scr", self.s_wd[r * 128:(r + 1) * 128, cc:cc + 512],
                    self.B[:, bo:bo + 512])
                si += 1
        for src, dst, R, Cc in [(self.inp["l1_d1"], self.s_d1, D, I8),
                                (self.inp["l1_d2"], self.s_d2, I8, D)]:
            for r in range(R // 128):
                for cc in range(0, Cc, 512):
                    tgt = self.load("ffA", self.W32[:, 0:512],
                                    src[r * 128:(r + 1) * 128, cc:cc + 512],
                                    ewaits=[("vector", sq.counts["vector"])])
                    bo = 512 * (si % 2)
                    dw = [("ffA", tgt)]
                    if (si % 2) in sslot:
                        dw.append(("scr", sslot[si % 2]))
                    def fc(h, s, bo=bo):
                        return h.tensor_copy(self.B[:, bo:bo + 512],
                                             self.W32[:, 0:512])
                    self.ve(fc, dma_waits=dw)
                    sslot[si % 2] = self.store(
                        "scr", dst[r * 128:(r + 1) * 128, cc:cc + 512],
                        self.B[:, bo:bo + 512])
                    si += 1

    def round_tile(self, dst, ya=0, nbits=8):
        def f05(h, s, ya=ya):
            return h.tensor_scalar(self.vf(1), self.vf(ya), 0.5, None, ALU.add)
        self.ve(f05)
        def fw(h, s, ya=ya):
            return h.tensor_scalar(self.vf(2), self.vf(ya), 128.5, None,
                                   ALU.add)
        self.ve(fw)
        dcur, dnxt = 2, 3
        for k in range(nbits - 1, -1, -1):
            def fb(h, s, k=k, dc=dcur, ya=ya):
                return h.tensor_scalar(self.vf(ya), self.vf(dc),
                                       float(2 ** k), float(2 ** k),
                                       ALU.is_ge, ALU.mult)
            self.ve(fb)
            def fd(h, s, dc=dcur, dn=dnxt, ya=ya):
                return h.tensor_tensor(self.vf(dn), self.vf(dc), self.vf(ya),
                                       ALU.subtract)
            self.ve(fd)
            dcur, dnxt = dnxt, dcur
        def fq(h, s, dst=dst, dc=dcur):
            return h.tensor_tensor(dst, self.vf(1), self.vf(dc), ALU.subtract)
        self.ve(fq)

    def ph_l0_ffn_a(self):
        for j in range(TSB):
            tsl = slice(j * 512, (j + 1) * 512)
            def fnb(h, s, tsl=tsl):
                return h.matmul(self.P4[:, 0:512], self.CONES[0:1, :],
                                self.RB[0:1, tsl], start=True, stop=True)
            self.pe(fnb)
            def fnc(h, s):
                return h.activation(self.MSW[:, 0:512], self.P4[:, 0:512],
                                    AF.Copy)
            self.se(fnc)
            for ch in range(DC):
                def f1(h, s, ch=ch, tsl=tsl):
                    return h.tensor_tensor(self.vf(0), self.xch(ch)[:, tsl],
                                           self.MSW[:, 0:512], ALU.add)
                self.ve(f1)
                def f2(h, s, ch=ch, tsl=tsl):
                    return h.tensor_tensor(self.ach(ch)[:, tsl], self.vf(0),
                                           self.RBC[:, tsl], ALU.mult)
                self.ve(f2)
        for ch in range(DC):
            for j in range(TSB):
                tsl = slice(j * 512, (j + 1) * 512)
                def fm(h, s, ch=ch, tsl=tsl):
                    return h.matmul(self.P3[:, 0:512], self.CHAD[:],
                                    self.ach(ch)[:, tsl], start=True, stop=True)
                self.pe(fm)
                def fc(h, s, ch=ch, tsl=tsl):
                    return h.activation(self.ach(ch)[:, tsl], self.P3[:, 0:512],
                                        AF.Copy)
                self.se(fc)
        perm = list(range(DC))
        spare = DC
        slot = lambda p: (self.HB[:, 0:N] if p == DC else self.ach(p))
        for st in range(4):
            b = 1 << st
            for a0 in range(DC):
                if a0 & b:
                    continue
                pa, pb_ = perm[a0], perm[a0 | b]
                def fbf(h, s, pa=pa, pb_=pb_, sp=spare):
                    h.tensor_tensor(slot(sp), slot(pa), slot(pb_), ALU.add)
                    return h.tensor_tensor(slot(pb_), slot(pa), slot(pb_),
                                           ALU.subtract)
                self.ve(fbf)
                perm[a0] = spare
                spare = pa
        if DC in perm:
            chx = perm.index(DC)
            def fmv(h, s, sp=spare):
                return h.tensor_copy(slot(sp), slot(DC))
            self.ve(fmv)
            perm[chx] = spare
        self.ffn_perm = tuple(perm)
        for j in range(TSB):
            tsl = slice(j * 512, (j + 1) * 512)
            def ft0(h, s, tsl=tsl, p0=perm[0]):
                return h.tensor_tensor(self.vf(0), slot(p0)[:, tsl],
                                       slot(p0)[:, tsl], ALU.mult)
            self.ve(ft0)
            for a0 in range(1, DC):
                def fti(h, s, tsl=tsl, p=perm[a0]):
                    h.tensor_tensor(self.vf(1), slot(p)[:, tsl],
                                    slot(p)[:, tsl], ALU.mult)
                    return h.tensor_tensor(self.vf(0), self.vf(0), self.vf(1),
                                           ALU.max)
                self.ve(fti)
            self._colmax_to_minv(j, C2048)
        for j in range(TSB):
            tsl = slice(j * 512, (j + 1) * 512)
            def fcst(h, s, tsl=tsl):
                return h.tensor_copy(self.MSW[:, 0:512], self.C[:, tsl])
            self.ve(fcst)
            def frc(h, s):
                return h.reciprocal(self.MSW[:, 512:1024], self.MSW[:, 0:512])
            self.ve(frc)
            for a0 in range(DC):
                p = perm[a0]
                def fy(h, s, p=p, tsl=tsl):
                    return h.scalar_tensor_tensor(self.vf(0), slot(p)[:, tsl],
                                                  C2048, self.MSW[:, 512:1024],
                                                  ALU.mult, ALU.mult)
                self.ve(fy)
                self.round_tile(slot(p)[:, tsl])

    def _colmax_to_minv(self, j, cnorm):
        tsl = slice(j * 512, (j + 1) * 512)
        def fcb(h, s):
            return h.tensor_copy(self.bsl(3), self.vf(0))
        self.ve(fcb)
        def ftr(h, s):
            for i in range(4):
                r = h.transpose(self.PTB[:, i * 128:(i + 1) * 128],
                                self.B[:, 1536 + i * 128:1536 + (i + 1) * 128],
                                self.CIDEN[:])
            return r
        self.pe(ftr)
        def frd(h, s):
            for i in range(4):
                r = h.reduce_max(self.SC[:, 8 + i:9 + i],
                                 self.PTB[:, i * 128:(i + 1) * 128], axis=AX.X)
            return r
        self.ve(frd)
        def fsq(h, s):
            return h.activation(self.SC[:, 12:16], self.SC[:, 8:12], AF.Sqrt)
        self.se(fsq)
        def fmv(h, s, cnorm=cnorm):
            return h.tensor_scalar(self.SC[:, 16:20], self.SC[:, 12:16],
                                   cnorm / 127.0, 1e-5 / 127.0, ALU.mult,
                                   ALU.max)
        self.ve(fmv)
        tgt = self.col_to_row(self.SC[:, 16:20], self.ROWF[0:1, 0:512])
        def fbb(h, s, tgt=tgt):
            return h.matmul(self.P4[:, 0:512], self.ONERF[0:1, :],
                            self.ROWF[0:1, 0:512], start=True, stop=True)
        self.pe(fbb, dma_waits=[("rcr", tgt)])
        def fcc(h, s, tsl=tsl):
            return h.activation(self.C[:, tsl], self.P4[:, 0:512], AF.Copy)
        self.se(fcc)

    def ph_l0_ffn_b(self):
        sq = self.seq
        perm = self.ffn_perm
        xs = lambda ch: self.ach(perm[ch])
        stW = {}
        for oc in range(IC):
            for wi, (wsrc, sci) in enumerate([(self.inp["l0_wg"], 0),
                                              (self.inp["l0_wu"], 2)]):
                rearr = wsrc[:, oc * 128:(oc + 1) * 128].rearrange(
                    "(c p) o -> p c o", p=128)
                dst3 = self.W32[:, 0:512].rearrange("p (c o) -> p c o", o=128)
                for q in range(4):
                    sl = rearr[:, q * 4:(q + 1) * 4, :]
                    tgt = self.load("ffw", dst3, sl,
                                    ewaits=[("vector", sq.counts["vector"])])
                    def fq(h, s, q=q, sci=sci):
                        h.scalar_tensor_tensor(self.vf(0), self.W32[:, 0:512],
                                               self.SC[:, sci + 1:sci + 2],
                                               self.Z512[:], ALU.mult, ALU.add)
                        h.tensor_scalar(self.vf(1), self.vf(0), 0.5, 1.0,
                                        ALU.is_ge, ALU.mult)
                        h.tensor_scalar(self.vf(2), self.vf(0), -0.5, 1.0,
                                        ALU.is_lt, ALU.mult)
                        return h.tensor_tensor(self.WT[:, q * 512:(q + 1) * 512],
                                               self.vf(1), self.vf(2),
                                               ALU.subtract)
                    self.ve(fq, dma_waits=[("ffw", tgt)])
                for j in range(TSB):
                    tsl = slice(j * 512, (j + 1) * 512)
                    def fp(h, s, tsl=tsl, j=j):
                        for ch in range(DC):
                            r = h.matmul(self.PB[j][:, 0:512],
                                         self.WT[:, ch * 128:(ch + 1) * 128],
                                         xs(ch)[:, tsl], start=(ch == 0),
                                         stop=(ch == DC - 1))
                        return r
                    self.pe(fp)
                for j in range(TSB):
                    tsl = slice(j * 512, (j + 1) * 512)
                    if wi == 0:
                        def fe(h, s, j=j, tsl=tsl):
                            return h.tensor_tensor(self.vf(0),
                                                   self.PB[j][:, 0:512],
                                                   self.C[:, tsl], ALU.mult)
                        self.ve(fe)
                        def fsl(h, s, j=j):
                            return h.activation(self.HB[:, j * 512:(j + 1) * 512],
                                                self.vf(0), AF.Silu,
                                                scale=self.SC[:, 0:1])
                        self.se(fsl)
                    else:
                        def fe(h, s, j=j, tsl=tsl):
                            return h.tensor_tensor(self.vf(0),
                                                   self.PB[j][:, 0:512],
                                                   self.C[:, tsl], ALU.mult)
                        self.ve(fe)
                        slot = (oc * 4 + j) % 2
                        bofs = 512 * slot
                        def fh(h, s, j=j, bofs=bofs):
                            return h.scalar_tensor_tensor(
                                self.B[:, bofs:bofs + 512], self.vf(0),
                                self.SC[:, 2:3],
                                self.HB[:, j * 512:(j + 1) * 512],
                                ALU.mult, ALU.mult)
                        dw = []
                        prev = stW.get(slot)
                        if prev is not None:
                            dw.append(("hst", prev))
                        self.ve(fh, dma_waits=dw)
                        stW[slot] = self.store(
                            "hst", self.s_h[oc * 128:(oc + 1) * 128, tsl],
                            self.B[:, bofs:bofs + 512])

    def ph_l0_ffn_c(self):
        sq = self.seq
        if self.trunc and self.DIAG:
            hstc = sq.dma_counts.get("hst", 0)
            self.load("hpr", self.bsl(3), self.s_h[0:128, 0:512],
                      dwaits=[("hst", hstc)])
            hc = sq.dma_counts["hpr"]
            def fcp(h, s):
                return h.tensor_copy(self.bsl(2), self.bsl(3))
            self.ve(fcp, dma_waits=[("hpr", hc)])
            self.store("dbg", self.dbg[:, 2 * DC * N + 4096:2 * DC * N + 4608],
                       self.bsl(2))
        for tb in range(TSB):
            tsl = slice(tb * 512, (tb + 1) * 512)
            hstc = sq.dma_counts.get("hst", 0)
            for ch in range(IC):
                self.load("hld", self.aslot512(ch),
                          self.s_h[ch * 128:(ch + 1) * 128, tsl],
                          dwaits=([("hst", hstc)] if ch == 0 else ()))
            hcnt = sq.dma_counts["hld"]
            for ch in range(IC):
                dw = [("hld", hcnt)] if ch == 0 else []
                def fm(h, s, ch=ch):
                    return h.matmul(self.P3[:, 0:512], self.CHAD[:],
                                    self.aslot512(ch), start=True, stop=True)
                self.pe(fm, dma_waits=dw)
                def fc(h, s, ch=ch):
                    return h.activation(self.aslot512(ch), self.P3[:, 0:512],
                                        AF.Copy)
                self.se(fc)
            perm = list(range(IC))
            spare = IC
            slot = lambda p: (self.HB[:, 0:512] if p == IC else self.aslot512(p))
            for st in range(6):
                b = 1 << st
                for a0 in range(IC):
                    if a0 & b:
                        continue
                    pa, pb_ = perm[a0], perm[a0 | b]
                    def fbf(h, s, pa=pa, pb_=pb_, sp=spare):
                        h.tensor_tensor(slot(sp), slot(pa), slot(pb_), ALU.add)
                        return h.tensor_tensor(slot(pb_), slot(pa), slot(pb_),
                                               ALU.subtract)
                    self.ve(fbf)
                    perm[a0] = spare
                    spare = pa
            def ft0(h, s, p0=perm[0]):
                return h.tensor_tensor(self.vf(0), slot(p0), slot(p0), ALU.mult)
            self.ve(ft0)
            for a0 in range(1, IC):
                def fti(h, s, p=perm[a0]):
                    h.tensor_tensor(self.vf(1), slot(p), slot(p), ALU.mult)
                    return h.tensor_tensor(self.vf(0), self.vf(0), self.vf(1),
                                           ALU.max)
                self.ve(fti)
            self._colmax_to_minv8()
            def frc(h, s):
                return h.reciprocal(self.MSW[:, 512:1024], self.MSW[:, 0:512])
            self.ve(frc)
            for a0 in range(IC):
                p = perm[a0]
                def fy(h, s, p=p):
                    return h.scalar_tensor_tensor(self.vf(0), slot(p), C8192,
                                                  self.MSW[:, 512:1024],
                                                  ALU.mult, ALU.mult)
                self.ve(fy)
                self.round_tile(slot(p))
            def fmd(h, s):
                return h.scalar_tensor_tensor(self.MSW[:, 512:1024],
                                              self.MSW[:, 0:512],
                                              self.SC[:, 4:5], self.Z512[:],
                                              ALU.mult, ALU.add)
            self.ve(fmd)
            for oc in range(DC):
                rearr = self.s_wd[:, oc * 128:(oc + 1) * 128].rearrange(
                    "(c p) o -> p c o", p=128)
                wt3 = self.WT[:, 0:2048].rearrange("p (c o) -> p c o", o=128)
                scrc = sq.dma_counts.get("scr", 0)
                for qtr in range(4):
                    sl = rearr[:, qtr * 16:(qtr + 1) * 16, :]
                    tgt = self.load("wbt", wt3, sl,
                                    ewaits=[("tensor", sq.counts["tensor"])],
                                    dwaits=[("scr", scrc)])
                    def fp(h, s, qtr=qtr, tgt=tgt, pp=tuple(perm)):
                        for ci in range(16):
                            ch = qtr * 16 + ci
                            r = h.matmul(self.P0[:, 0:512],
                                         self.WT[:, ci * 128:(ci + 1) * 128],
                                         slot(pp[ch]),
                                         start=(qtr == 0 and ci == 0),
                                         stop=(qtr == 3 and ci == 15))
                        return r
                    self.pe(fp, dma_waits=[("wbt", tgt)])
                def fe(h, s):
                    return h.tensor_tensor(self.vf(0), self.P0[:, 0:512],
                                           self.MSW[:, 512:1024], ALU.mult)
                self.ve(fe)
                def fa(h, s, oc=oc, tsl=tsl):
                    return h.tensor_tensor(self.xch(oc)[:, tsl],
                                           self.xch(oc)[:, tsl], self.vf(0),
                                           ALU.add)
                self.ve(fa)

    def _colmax_to_minv8(self):
        def fcb(h, s):
            return h.tensor_copy(self.bsl(3), self.vf(0))
        self.ve(fcb)
        def ftr(h, s):
            for i in range(4):
                r = h.transpose(self.PTB[:, i * 128:(i + 1) * 128],
                                self.B[:, 1536 + i * 128:1536 + (i + 1) * 128],
                                self.CIDEN[:])
            return r
        self.pe(ftr)
        def frd(h, s):
            for i in range(4):
                r = h.reduce_max(self.SC[:, 8 + i:9 + i],
                                 self.PTB[:, i * 128:(i + 1) * 128], axis=AX.X)
            return r
        self.ve(frd)
        def fsq(h, s):
            return h.activation(self.SC[:, 12:16], self.SC[:, 8:12], AF.Sqrt)
        self.se(fsq)
        def fmv(h, s):
            return h.tensor_scalar(self.SC[:, 16:20], self.SC[:, 12:16],
                                   C8192 / 127.0, 1e-5 / 127.0, ALU.mult,
                                   ALU.max)
        self.ve(fmv)
        tgt = self.col_to_row(self.SC[:, 16:20], self.ROWF[0:1, 0:512])
        def fbb(h, s, tgt=tgt):
            return h.matmul(self.P4[:, 0:512], self.ONERF[0:1, :],
                            self.ROWF[0:1, 0:512], start=True, stop=True)
        self.pe(fbb, dma_waits=[("rcr", tgt)])
        def fcc(h, s):
            return h.activation(self.MSW[:, 0:512], self.P4[:, 0:512], AF.Copy)
        self.se(fcc)

    def ph_l1_attn(self):
        for u in range(-3, 5):
            ci = u + 3
            if u >= 0:
                qs, ks = slice(128 * u, 128 * u + 512), slice(0, 128)
            else:
                qs, ks = slice(0, 512), slice(-128 * u, -128 * u + 128)
            def fdd(h, s, qs=qs, ks=ks):
                h.matmul(self.P4[:, 0:512], self.ONERF[0:1, :],
                         self.POS[0:1, qs], start=True, stop=False)
                return h.matmul(self.P4[:, 0:512], self.POS[0:1, ks],
                                self.NEG1[0:1, :], start=False, stop=True)
            self.pe(fdd)
            def frl(h, s):
                return h.tensor_scalar(self.vf(0), self.P4[:, 0:512], 0.0,
                                       None, ALU.max)
            self.ve(frl)
            def fsq(h, s, ci=ci):
                return h.activation(self.C[:, ci * 512:(ci + 1) * 512],
                                    self.vf(0), AF.Sqrt)
            self.se(fsq)
        for g in range(G1):
            self.load_wtile("wp", self.inp["l1_wk"], g * 128, wsum=True)
            for j in range(TSB):
                tsl = slice(j * 512, (j + 1) * 512)
                self.proj_psum(self.P0, tsl)
                def fk(h, s, tsl=tsl):
                    return h.tensor_tensor(self.HB[:, tsl], self.P0[:, 0:512],
                                           self.RBC[:, tsl], ALU.mult)
                self.ve(fk)
            self.load_wtile("wp", self.inp["l1_wv"], g * 128, wsum=True)
            for j in range(TSB):
                tsl = slice(j * 512, (j + 1) * 512)
                self.proj_psum(self.P0, tsl)
                def fv(h, s, tsl=tsl):
                    return h.tensor_tensor(self.bsl(3), self.P0[:, 0:512],
                                           self.RBC[:, tsl], ALU.mult)
                self.ve(fv)
                def ftr(h, s):
                    for i in range(4):
                        r = h.transpose(self.PTB[:, i * 128:(i + 1) * 128],
                                        self.B[:, 1536 + i * 128:1536 + (i + 1) * 128],
                                        self.CIDEN[:])
                    return r
                self.pe(ftr)
                def fvc(h, s, j=j):
                    return h.activation(self.HB[:, 2048 + j * 512:2048 + (j + 1) * 512],
                                        self.PTB[:, 0:512], AF.Copy)
                self.se(fvc)
            for rr in range(4):
                hd = g * 4 + rr
                slope = 2.0 ** (-(hd + 1) / 2.0)
                self.load_wtile("wp", self.inp["l1_wq"], hd * 128, scale=QSC,
                                wsum=True)
                for qb in range(TSB):
                    qsl = slice(qb * 512, (qb + 1) * 512)
                    self.proj_psum(self.P0, qsl)
                    def fq(h, s, qsl=qsl):
                        return h.tensor_tensor(self.bsl(0), self.P0[:, 0:512],
                                               self.RBC[:, qsl], ALU.mult)
                    self.ve(fq)
                    nkt = (qb + 1) * 4
                    for kt in range(nkt):
                        u = 4 * qb - kt
                        def fsc(h, s, kt=kt):
                            return h.matmul(self.P2[:, 0:512],
                                            self.HB[:, kt * 128:(kt + 1) * 128],
                                            self.bsl(0), start=True, stop=True)
                        self.pe(fsc)
                        if u <= 4:
                            sqd = self.C[:, (u + 3) * 512:(u + 4) * 512]
                        else:
                            qs = slice(128 * u, 128 * u + 512)
                            def fdd(h, s, qs=qs, kt=kt):
                                h.matmul(self.P4[:, 0:512], self.ONERF[0:1, :],
                                         self.POS[0:1, qs], start=True,
                                         stop=False)
                                return h.matmul(self.P4[:, 0:512],
                                                self.POS[0:1, 0:128],
                                                self.NEG1[0:1, :], start=False,
                                                stop=True)
                            self.pe(fdd)
                            def fsqg(h, s):
                                return h.activation(self.bsl(3),
                                                    self.P4[:, 0:512], AF.Sqrt)
                            self.se(fsqg)
                            sqd = self.bsl(3)
                        def fz(h, s, sqd=sqd, slope=slope):
                            return h.scalar_tensor_tensor(self.vf(1), sqd,
                                                          -slope,
                                                          self.P2[:, 0:512],
                                                          ALU.mult, ALU.add)
                        self.ve(fz)
                        if u <= 0:
                            mo = 384 - 128 * (-u)
                            def fm1(h, s, mo=mo):
                                return h.scalar_tensor_tensor(
                                    self.vf(0), self.CMK[:, mo:mo + 512], 1e5,
                                    self.vf(1), ALU.mult, ALU.add)
                            self.ve(fm1)
                            def fm2(h, s):
                                return h.tensor_scalar(self.vf(1), self.vf(0),
                                                       -1e5, None, ALU.add)
                            self.ve(fm2)
                        def fe(h, s):
                            return h.activation(self.bsl(1), self.vf(1), AF.Exp)
                        self.se(fe)
                        def fden(h, s, kt=kt, nkt=nkt):
                            return h.matmul(self.PR[0:1, 0:512],
                                            self.CONES[:, 0:1], self.bsl(1),
                                            start=(kt == 0), stop=(kt == nkt - 1))
                        self.pe(fden)
                        def fpv(h, s, kt=kt, nkt=nkt):
                            return h.matmul(self.P3[:, 0:512],
                                            self.HB[:, 2048 + kt * 128:2048 + (kt + 1) * 128],
                                            self.bsl(1), start=(kt == 0),
                                            stop=(kt == nkt - 1))
                        self.pe(fpv)
                    def frc(h, s):
                        return h.reciprocal(self.ROWF[0:1, 0:512],
                                            self.PR[0:1, 0:512])
                    self.ve(frc)
                    def fb(h, s):
                        return h.matmul(self.P4[:, 0:512], self.ONERF[0:1, :],
                                        self.ROWF[0:1, 0:512], start=True,
                                        stop=True)
                    self.pe(fb)
                    def fcp(h, s):
                        return h.activation(self.vf(0), self.P4[:, 0:512],
                                            AF.Copy)
                    self.se(fcp)
                    def fo(h, s, hd=hd, qsl=qsl):
                        return h.tensor_tensor(self.ach(hd)[:, qsl],
                                               self.P3[:, 0:512], self.vf(0),
                                               ALU.mult)
                    self.ve(fo)
        self._wo_residual(self.inp["l1_wo"])

    def ph_l1_ffn(self):
        sq = self.seq
        GI, GB = 6 * DC, 0
        gcol = lambda k: self.GAM[:, k * DC:(k + 1) * DC]
        self.ph_stats(eps_col=0)
        for oc in range(DC):
            self.load_wtile("gw", self.inp["l1_gw"], oc * 128, wsum=True)
            for j in range(TSB):
                tsl = slice(j * 512, (j + 1) * 512)
                self.proj_psum(self.P0, tsl, fold=True)
                def fgp(h, s, tsl=tsl):
                    return h.tensor_tensor(self.vf(0), self.P0[:, 0:512],
                                           self.RBC[:, tsl], ALU.mult)
                self.ve(fgp)
                def fg(h, s, oc=oc, tsl=tsl):
                    return h.activation(self.ach(oc)[:, tsl], self.vf(0),
                                        AF.Sigmoid,
                                        bias=gcol(6)[:, oc:oc + 1])
                self.se(fg)
        def fxl(h, s):
            src = self.X[:].rearrange("p (c n) -> p c n", n=N)[:, :, N - 1:N]
            return h.tensor_copy(self.SC[:, 32:48],
                                 src.rearrange("p c n -> p (c n)"))
        self.ve(fxl)
        def fzk(h, s):
            h.matmul(self.P4[:, 0:1], self.CONES[0:1, :],
                     self.RB[0:1, N - 1:N], start=True, stop=True)
            return h.matmul(self.P4[:, 1:2], self.CONES[0:1, :],
                            self.RB[0:1, 2 * N - 1:2 * N], start=True,
                            stop=True)
        self.pe(fzk)
        def fzc(h, s):
            return h.activation(self.SC[:, 28:30], self.P4[:, 0:2], AF.Copy)
        self.se(fzc)
        def fz1(h, s):
            return h.scalar_tensor_tensor(self.SC[:, 48:64], self.SC[:, 32:48],
                                          self.SC[:, 28:29],
                                          self.Z512[:, 0:16], ALU.add, ALU.add)
        self.ve(fz1)
        def fz2(h, s):
            return h.scalar_tensor_tensor(self.SC[:, 32:48], self.SC[:, 48:64],
                                          self.SC[:, 29:30],
                                          self.Z512[:, 0:16], ALU.mult,
                                          ALU.add)
        self.ve(fz2)
        def fs1r(h, s):
            return h.reduce_sum(self.SC[:, 25:26], self.SC[:, 32:48], axis=AX.X)
        self.ve(fs1r)
        def fs1(h, s):
            return h.matmul(self.PR[0:1, 0:1], self.ONE1F[:],
                            self.SC[:, 25:26], start=True, stop=True)
        self.pe(fs1)
        def fq2(h, s):
            h.tensor_tensor(self.SC[:, 48:64], self.SC[:, 32:48],
                            self.SC[:, 32:48], ALU.mult)
            return h.reduce_sum(self.SC[:, 24:25], self.SC[:, 48:64], axis=AX.X)
        self.ve(fq2)
        def fs2(h, s):
            return h.matmul(self.PR[32:33, 0:1], self.ONE1F[:],
                            self.SC[:, 24:25], start=True, stop=True)
        self.pe(fs2)
        self._vec_ln_cols(self.SC[:, 32:48], self.SCB[:, 16:32],
                          gcol(0), gcol(1), eps_col=1,
                          dsum=self.PR[0:1, 0:1], dss=self.PR[32:33, 0:1])
        self._colproj(self.inp["l1_agg"], self.SCB[:, 16:32], self.THC,
                      self.AGGBC, "agg")
        if self.trunc:
            def fdg(h, s):
                h.tensor_copy(self.B[:, 32:48], self.THC[:])
                h.tensor_copy(self.B[:, 48:64], self.SCB[:, 16:32])
                h.tensor_copy(self.B[:, 64:68], self.SC[:, 20:24])
                return h.tensor_copy(self.B[:, 68:84], self.SC[:, 32:48])
            self.ve(fdg)
        for step in range(4):
            def fq2b(h, s):
                h.tensor_tensor(self.SC[:, 48:64], self.THC[:],
                                self.THC[:], ALU.mult)
                h.reduce_sum(self.SC[:, 24:25], self.SC[:, 48:64], axis=AX.X)
                return h.reduce_sum(self.SC[:, 25:26], self.THC[:], axis=AX.X)
            self.ve(fq2b)
            def fs2b(h, s):
                h.matmul(self.PR[0:1, 0:1], self.ONE1F[:], self.SC[:, 25:26],
                         start=True, stop=True)
                return h.matmul(self.PR[32:33, 0:1], self.ONE1F[:],
                                self.SC[:, 24:25], start=True, stop=True)
            self.pe(fs2b)
            self._vec_ln_cols(self.THC[:], self.SCB[:, 0:16], gcol(2), gcol(3),
                              eps_col=1, dsum=self.PR[0:1, 0:1],
                              dss=self.PR[32:33, 0:1])
            for os in range(16):
                osl = slice(os * 512, (os + 1) * 512)
                scrc = sq.dma_counts.get("scr", 0)
                for ch in range(DC):
                    tgt = self.load("dld", self.WT[:, (ch % 4) * 512:(ch % 4 + 1) * 512],
                                    self.s_d1[ch * 128:(ch + 1) * 128, osl],
                                    ewaits=[("tensor", sq.counts["tensor"])],
                                    dwaits=[("scr", scrc)])
                    def fp(h, s, ch=ch, tgt=tgt):
                        return h.matmul(self.PR[0:1, 0:512],
                                        self.SCB[:, ch:ch + 1],
                                        self.WT[:, (ch % 4) * 512:(ch % 4 + 1) * 512],
                                        start=(ch == 0), stop=(ch == DC - 1))
                    self.pe(fp, dma_waits=[("dld", tgt)])
                tb = self.load("dlb", self.ROWF[0:1, 512:1024],
                               self.inp["l1_d1b"][0:1, osl],
                               ewaits=[("vector", sq.counts["vector"])])
                def fba(h, s):
                    return h.tensor_tensor(self.ROWF[0:1, 0:512],
                                           self.PR[0:1, 0:512],
                                           self.ROWF[0:1, 512:1024], ALU.add)
                self.ve(fba, dma_waits=[("dlb", tb)])
                def fgl(h, s):
                    return h.activation(self.RB[0:1, 4096:4608],
                                        self.ROWF[0:1, 0:512], AF.Gelu)
                self.se(fgl)
                pms = sq.dma_counts.get("msc", 0)
                t1 = self.store("rcw", self.s_rr, self.RB[0:1, 4096:4608],
                                dwaits=[("msc", pms)] if pms else ())
                self.load("msc",
                          self.SCB[:, 32 + os * 4:32 + (os + 1) * 4],
                          self.s_rr.rearrange("a (c p) -> p c", p=128),
                          dwaits=[("rcw", t1)], slow=True)
            mscc = sq.dma_counts.get("msc", 0)
            def fbar(h, s):
                return h.memset(self.ROWF[0:1, 1020:1021], 0.0)
            self.ve(fbar, dma_waits=[("msc", mscc)])
            for os2 in range(4):
                osl = slice(os2 * 512, (os2 + 1) * 512)
                scrc = sq.dma_counts.get("scr", 0)
                for ch in range(IC):
                    tgt = self.load("dld", self.WT[:, (ch % 4) * 512:(ch % 4 + 1) * 512],
                                    self.s_d2[ch * 128:(ch + 1) * 128, osl],
                                    ewaits=[("tensor", sq.counts["tensor"])],
                                    dwaits=[("scr", scrc)])
                    def fp(h, s, ch=ch, tgt=tgt):
                        return h.matmul(self.PR[0:1, 0:512],
                                        self.SCB[:, 32 + ch:33 + ch],
                                        self.WT[:, (ch % 4) * 512:(ch % 4 + 1) * 512],
                                        start=(ch == 0), stop=(ch == IC - 1))
                    self.pe(fp, dma_waits=[("dld", tgt)])
                tb = self.load("dlb", self.ROWF[0:1, 512:1024],
                               self.inp["l1_d2b"][0:1, osl],
                               ewaits=[("vector", sq.counts["vector"])])
                def fba(h, s):
                    return h.tensor_tensor(self.ROWF[0:1, 0:512],
                                           self.PR[0:1, 0:512],
                                           self.ROWF[0:1, 512:1024], ALU.add)
                self.ve(fba, dma_waits=[("dlb", tb)])
                pms = sq.dma_counts.get("msc", 0)
                t1 = self.store("rcw", self.s_rf, self.ROWF[0:1, 0:512],
                                dwaits=[("msc", pms)] if pms else ())
                tgu = self.load("msc", self.SC[:, 24:28],
                                self.s_rf.rearrange("a (c p) -> p c", p=128),
                                dwaits=[("rcw", t1)], slow=True)
                def fup(h, s, os2=os2, tgu=tgu):
                    return h.tensor_tensor(self.THC[:, os2 * 4:(os2 + 1) * 4],
                                           self.THC[:, os2 * 4:(os2 + 1) * 4],
                                           self.SC[:, 24:28], ALU.add)
                self.ve(fup, dma_waits=[("msc", tgu)])
        def fct(h, s):
            return h.tensor_copy(self.SCB[:, 0:16], self.THC[:])
        self.ve(fct)
        self._colproj(self.inp["l1_bc"], self.SCB[:, 0:16], self.BCC,
                      self.BCBC, "bcw")
        for j in range(TSB):
            tsl = slice(j * 512, (j + 1) * 512)
            def fnb(h, s, tsl=tsl):
                return h.matmul(self.P4[:, 0:512], self.CONES[0:1, :],
                                self.RB[0:1, tsl], start=True, stop=True)
            self.pe(fnb)
            def fnc(h, s):
                return h.activation(self.MSW[:, 0:512], self.P4[:, 0:512],
                                    AF.Copy)
            self.se(fnc)
            for ch in range(DC):
                def fz1(h, s, ch=ch, tsl=tsl):
                    return h.tensor_tensor(self.vf(0), self.xch(ch)[:, tsl],
                                           self.MSW[:, 0:512], ALU.add)
                self.ve(fz1)
                def fz2(h, s, tsl=tsl):
                    return h.tensor_tensor(self.vf(1), self.vf(0),
                                           self.RBC[:, tsl], ALU.mult)
                self.ve(fz2)
                def fgb(h, s, ch=ch, tsl=tsl):
                    return h.scalar_tensor_tensor(self.ach(ch)[:, tsl],
                                                  self.ach(ch)[:, tsl],
                                                  self.BCC[:, ch:ch + 1],
                                                  self.vf(1), ALU.mult,
                                                  ALU.add)
                self.ve(fgb)
        self.ph_stats(eps_col=1, src=self.ach)
        for j in range(TSB):
            tsl = slice(j * 512, (j + 1) * 512)
            def fnb(h, s, tsl=tsl):
                return h.matmul(self.P4[:, 0:512], self.CONES[0:1, :],
                                self.RB[0:1, tsl], start=True, stop=True)
            self.pe(fnb)
            def fnc(h, s):
                return h.activation(self.MSW[:, 0:512], self.P4[:, 0:512],
                                    AF.Copy)
            self.se(fnc)
            for ch in range(DC):
                def f1(h, s, ch=ch, tsl=tsl):
                    return h.tensor_tensor(self.vf(0), self.ach(ch)[:, tsl],
                                           self.MSW[:, 0:512], ALU.add)
                self.ve(f1)
                def f2(h, s, tsl=tsl):
                    return h.tensor_tensor(self.vf(1), self.vf(0),
                                           self.RBC[:, tsl], ALU.mult)
                self.ve(f2)
                def f3a(h, s, ch=ch):
                    return h.activation(self.vf(3), self.vf(1), AF.Copy,
                                        scale=gcol(4)[:, ch:ch + 1])
                self.se(f3a)
                prev_out = sq.dma_counts.get("out", 0)
                def f3(h, s, ch=ch, tsl=tsl):
                    return h.scalar_tensor_tensor(self.vf(2), self.vf(3),
                                                  gcol(5)[:, ch:ch + 1],
                                                  self.xch(ch)[:, tsl],
                                                  ALU.add, ALU.add)
                self.ve(f3, dma_waits=([("out", prev_out)] if prev_out else []))
                self.store("out", self.y_t[ch * 128:(ch + 1) * 128, tsl],
                           self.vf(2))

    def _vec_ln_cols(self, src, dstb, gcol_g, gcol_b, eps_col, dsum, dss):
        epsv = 1e-5 if eps_col == 0 else 1e-6
        def fa(h, s, dsum=dsum):
            return h.tensor_scalar(self.ROWF[0:1, 4:5], dsum, 1.0 / D, None,
                                   ALU.mult)
        self.ve(fa)
        def fb(h, s, dss=dss):
            return h.tensor_scalar(self.ROWF[0:1, 5:6], dss, 1.0 / D, None,
                                   ALU.mult)
        self.ve(fb)
        def fc(h, s):
            h.matmul(self.P4[:, 0:1], self.ONERF[0:1, :], self.ROWF[0:1, 4:5],
                     start=True, stop=True)
            return h.matmul(self.P4[:, 1:2], self.ONERF[0:1, :],
                            self.ROWF[0:1, 5:6], start=True, stop=True)
        self.pe(fc)
        def fd(h, s):
            return h.activation(self.SC[:, 20:22], self.P4[:, 0:2], AF.Copy)
        self.se(fd)
        def fe(h, s):
            return h.tensor_tensor(self.SC[:, 22:23], self.SC[:, 20:21],
                                   self.SC[:, 20:21], ALU.mult)
        self.ve(fe)
        def ff(h, s):
            return h.tensor_tensor(self.SC[:, 23:24], self.SC[:, 21:22],
                                   self.SC[:, 22:23], ALU.subtract)
        self.ve(ff)
        def fg(h, s, epsv=epsv):
            return h.tensor_scalar(self.SC[:, 22:23], self.SC[:, 23:24], epsv,
                                   None, ALU.add)
        self.ve(fg)
        def fh(h, s):
            return h.reciprocal(self.SC[:, 23:24], self.SC[:, 22:23])
        self.ve(fh)
        def fi(h, s):
            return h.activation(self.SC[:, 21:22], self.SC[:, 23:24], AF.Sqrt)
        self.se(fi)
        def f1(h, s, src=src):
            return h.scalar_tensor_tensor(self.SC[:, 48:64], src,
                                          self.SC[:, 20:21],
                                          self.Z512[:, 0:16], ALU.subtract,
                                          ALU.add)
        self.ve(f1)
        def f2(h, s):
            return h.scalar_tensor_tensor(self.SC[:, 24:40], self.SC[:, 48:64],
                                          self.SC[:, 21:22],
                                          self.Z512[:, 0:16], ALU.mult,
                                          ALU.add)
        self.ve(f2)
        def f3(h, s, gg=gcol_g):
            return h.tensor_tensor(self.SC[:, 48:64], self.SC[:, 24:40], gg,
                                   ALU.mult)
        self.ve(f3)
        def f4(h, s, gb=gcol_b, dstb=dstb):
            return h.tensor_tensor(dstb, self.SC[:, 48:64], gb, ALU.add)
        self.ve(f4)

    def _colproj(self, wsrc, xcol_b, dstcol, bias_col, fam):
        sq = self.seq
        for osl in range(4):
            csl = slice(osl * 512, (osl + 1) * 512)
            for ch in range(DC):
                tgt = self.load(fam, self.W32[:, 0:512],
                                wsrc[ch * 128:(ch + 1) * 128, csl],
                                ewaits=[("vector", sq.counts["vector"])])
                def fc(h, s, ch=ch):
                    return h.tensor_copy(self.WT[:, (ch % 4) * 512:(ch % 4 + 1) * 512],
                                         self.W32[:, 0:512])
                self.ve(fc, dma_waits=[(fam, tgt)])
                def fp(h, s, ch=ch, xcol_b=xcol_b):
                    return h.matmul(self.PR[0:1, 0:512], xcol_b[:, ch:ch + 1],
                                    self.WT[:, (ch % 4) * 512:(ch % 4 + 1) * 512],
                                    start=(ch == 0), stop=(ch == DC - 1))
                self.pe(fp)
            def fcr(h, s):
                return h.tensor_copy(self.ROWF[0:1, 0:512], self.PR[0:1, 0:512])
            self.ve(fcr)
            pms = sq.dma_counts.get("msc", 0)
            t1 = self.store("rcw", self.s_rf, self.ROWF[0:1, 0:512],
                            dwaits=[("msc", pms)] if pms else ())
            tgu = self.load("msc", self.SC[:, 24:28],
                            self.s_rf.rearrange("a (c p) -> p c", p=128),
                            dwaits=[("rcw", t1)], slow=True)
            def fadd(h, s, osl=osl, dstcol=dstcol, bias_col=bias_col):
                return h.tensor_tensor(dstcol[:, osl * 4:(osl + 1) * 4],
                                       self.SC[:, 24:28],
                                       bias_col[:, osl * 4:(osl + 1) * 4],
                                       ALU.add)
            self.ve(fadd, dma_waits=[("msc", tgu)])

    DIAG = False

    def ph_dump5(self):
        def fc1(h, s):
            h.tensor_copy(self.B[:, 0:16], self.THC[:])
            return h.tensor_copy(self.B[:, 16:32], self.BCC[:])
        self.ve(fc1)
        self.store("dbg", self.dbg[:, 2 * DC * N + 4096:2 * DC * N + 4192],
                   self.B[:, 0:96])

    def ph_dump(self):
        if self.DIAG:
            def fsc(h, s):
                return h.tensor_copy(self.B[:, 0:64], self.SC[:])
            self.ve(fsc)
            self.store("dbg", self.dbg[:, 2 * DC * N:2 * DC * N + 2048],
                       self.HB[:, 0:2048])
            self.store("dbg", self.dbg[:, 2 * DC * N + 2048:2 * DC * N + 4096],
                       self.WT[:])
            self.store("dbg", self.dbg[:, 2 * DC * N + 4096:2 * DC * N + 4160],
                       self.B[:, 0:64])
        for ch in range(DC):
            self.store("dbg", self.dbg[:, ch * N:(ch + 1) * N], self.xch(ch))
        for ch in range(DC):
            self.store("dbg", self.dbg[:, (DC + ch) * N:(DC + ch + 1) * N],
                       self.ach(ch))
        if not self.DIAG:
            self.store("dbg", self.dbg[:, 2 * DC * N:2 * DC * N + 4096],
                       self.C[:])

    def finish(self):
        fams = [f for f in self.seq.dma_counts if f in ("out", "dbg")]
        def fw(h, s, fams=tuple(fams)):
            r = None
            for f in fams:
                r = h.wait_ge(s.dsem[f], self.seq.dma_counts[f])
            return r
        self.seq.step("sync", fw, auto=True, inc=False)

    def build(self):
        p = self.phases
        self.ph_init()
        if p >= 1:
            self.ph_stats(eps_col=0)
            self.ph_l0_attn()
        if p >= 2:
            self.ph_prepass()
            self.ph_stats(eps_col=0)
            self.ph_l0_ffn_a()
        if p >= 3:
            self.ph_l0_ffn_b()
            self.ph_l0_ffn_c()
        if p >= 4:
            self.ph_stats(eps_col=0)
            self.ph_l1_attn()
        if p >= 5:
            self.ph_l1_ffn()
        if self.trunc:
            if p >= 5:
                self.ph_dump5()
            self.ph_dump()
        self.finish()
        self.emit()
        return self.nc

    def emit(self):
        nc, seq = self.nc, self.seq
        es = self.es
        class S:
            pass
        sems = S()
        sems.dsem = {}
        for f in seq.dma_counts:
            sems.dsem[f] = es.enter_context(nc.semaphore("d_" + f))
        esem = {e: es.enter_context(nc.semaphore("e_" + e))
                for e in ENGS if e != "sync"}
        block = es.enter_context(nc.Block())
        per_eng = {e: [] for e in ENGS}
        for it in seq.items:
            per_eng[it[0]].append(it)

        _OPS = ("tensor_tensor", "tensor_scalar", "scalar_tensor_tensor",
                "tensor_copy", "memset", "reduce_sum", "reduce_max",
                "reciprocal", "activation", "copy", "tensor_reduce",
                "stream_shuffle")

        class DrainWrap:
            def __init__(self, h):
                self._h = h

            def __getattr__(self, nm):
                f = getattr(self._h, nm)
                if nm not in _OPS:
                    return f
                def g(*a, **k):
                    r = f(*a, **k)
                    self._h.drain()
                    return r
                return g

        def run(handle, eng):
            if eng in ("vector", "scalar"):
                handle = DrainWrap(handle)
            for (e, waits, dw, fn, inc) in per_eng[eng]:
                for (o, cnt) in waits:
                    handle.wait_ge(esem[o], cnt)
                for (f, cnt) in dw:
                    handle.wait_ge(sems.dsem[f], cnt)
                r = fn(handle, sems)
                if inc:
                    if r is None:
                        raise RuntimeError(f"step on {eng} returned None")
                    r.then_inc(esem[eng], 1)

        @block.tensor
        def _(h):
            run(h, "tensor")

        @block.vector
        def _(h):
            run(h, "vector")

        @block.scalar
        def _(h):
            run(h, "scalar")

        @block.sync
        def _(h):
            run(h, "sync")

        es.close()


def _consts():
    ii = np.arange(128)
    had = np.where((ii[:, None] & ii[None, :] & 0x7F).astype(np.int64) >= 0, 0, 0)
    pop = np.zeros((128, 128), dtype=np.int64)
    a = ii[:, None] & ii[None, :]
    for b in range(7):
        pop += (a >> b) & 1
    had = np.where(pop % 2 == 0, 1.0, -1.0).astype(BF)
    iden = np.eye(128, dtype=BF)
    ones = np.ones((128, 128), dtype=BF)
    kk = np.arange(128)[:, None]
    cc = np.arange(896)[None, :]
    mk = (kk <= cc - 384).astype(np.float32).astype(BF)
    pos = np.arange(N, dtype=np.float32).reshape(1, N)
    j64 = (np.arange(128) % 64).astype(np.float32).reshape(128, 1)
    neg1 = np.full((1, 512), -1.0, dtype=np.float32)
    onesf = np.ones((128, 1), dtype=np.float32)
    onerf = np.ones((1, 128), dtype=np.float32)
    return dict(c_had=np.ascontiguousarray(had), c_iden=iden, c_ones=ones,
                c_mk=np.ascontiguousarray(mk), c_pos=pos, c_j64=j64,
                c_neg1=neg1, c_onesf=onesf, c_onerf=onerf)


def _col16(v):
    return np.ascontiguousarray(v.reshape(DC, 128).T.astype(np.float32))


_CACHE = {}


def kernel(L0_attn_x, L0_attn_Wq, L0_attn_Wk, L0_attn_Wv, L0_attn_Wo,
           L0_ffn_w_gate, L0_ffn_w_up, L0_ffn_w_down,
           L1_attn_Wq, L1_attn_Wk, L1_attn_Wv, L1_attn_Wo,
           L1_ffn_in_gamma, L1_ffn_in_beta, L1_ffn_agg_w, L1_ffn_agg_b,
           L1_ffn_pn_gamma, L1_ffn_pn_beta, L1_ffn_d1_w, L1_ffn_d1_b,
           L1_ffn_d2_w, L1_ffn_d2_b, L1_ffn_bc_w, L1_ffn_bc_b,
           L1_ffn_gate_w, L1_ffn_gate_b, L1_ffn_out_gamma, L1_ffn_out_beta):
    asc = np.ascontiguousarray
    f32 = lambda v: asc(np.asarray(v, dtype=np.float32))
    gam = np.stack([_col16(f32(v)) for v in
                    (L1_ffn_in_gamma, L1_ffn_in_beta, L1_ffn_pn_gamma,
                     L1_ffn_pn_beta, L1_ffn_out_gamma, L1_ffn_out_beta,
                     L1_ffn_gate_b)], axis=0)
    gam = asc(np.concatenate([g for g in gam], axis=1))
    shared = dict(
        l0_wq=asc(f32(L0_attn_Wq).T), l0_wk=asc(f32(L0_attn_Wk).T),
        l0_wv=asc(f32(L0_attn_Wv).T), l0_wo=asc(f32(L0_attn_Wo).T),
        l0_wg=asc(f32(L0_ffn_w_gate).T), l0_wu=asc(f32(L0_ffn_w_up).T),
        l0_wd=asc(f32(L0_ffn_w_down).T),
        l1_wq=asc(f32(L1_attn_Wq).T), l1_wk=asc(f32(L1_attn_Wk).T),
        l1_wv=asc(f32(L1_attn_Wv).T), l1_wo=asc(f32(L1_attn_Wo).T),
        l1_agg=f32(L1_ffn_agg_w), l1_bc=f32(L1_ffn_bc_w),
        l1_gw=f32(L1_ffn_gate_w),
        l1_d1=f32(L1_ffn_d1_w), l1_d2=f32(L1_ffn_d2_w),
        l1_d1b=f32(L1_ffn_d1_b).reshape(1, I8),
        l1_d2b=f32(L1_ffn_d2_b).reshape(1, D),
        gam=gam, aggb_c=_col16(f32(L1_ffn_agg_b)),
        bcb_c=_col16(f32(L1_ffn_bc_b)),
        **_consts())
    x = f32(L0_attn_x)
    key = ("prog", PHASES)
    if key not in _CACHE:
        _CACHE[key] = Prog(PHASES).build()
    nc = _CACHE[key]
    in_maps = []
    for b in range(8):
        m = dict(shared)
        m["x_t"] = asc(x[b].T)
        in_maps.append(m)
    res = run_bass_kernel_spmd(nc, in_maps, core_ids=list(range(8)))
    out = np.stack([np.asarray(res.results[b]["y_t"]).T for b in range(8)],
                   axis=0)
    return np.ascontiguousarray(out.astype(np.float32))


# version 05 (final)
# speedup vs baseline: 1.0110x; 1.0110x over previous
import math
import numpy as np
import ml_dtypes
from contextlib import ExitStack
import concourse.bass as bass
import concourse.mybir as mybir
from concourse.bass_utils import run_bass_kernel_spmd

F32 = mybir.dt.float32
BF16 = mybir.dt.bfloat16
AF = mybir.ActivationFunctionType
ALU = mybir.AluOpType
AX = mybir.AxisListType
BF = ml_dtypes.bfloat16

N = 2048
D = 2048
DC = 16
I8 = 8192
IC = 64
HEADS = 16
G1 = 4
TSB = 4
TWO_PI = 6.283185307179586
PI = 3.141592653589793
LN1E4 = 9.210340371976184
C2048 = 1.0 / math.sqrt(2048.0)
C8192 = 1.0 / math.sqrt(8192.0)
QSC = 1.0 / math.sqrt(128.0)

ENGS = ("tensor", "vector", "scalar", "sync")
PHASES = 99


class Seq:

    def __init__(self):
        self.items = []
        self.counts = {e: 0 for e in ENGS}
        self.seen = {e: {o: 0 for o in ENGS} for e in ENGS}
        self.dma_counts = {}
        self.dma_seen = {e: {} for e in ENGS}

    def dma_tick(self, fam):
        self.dma_counts[fam] = self.dma_counts.get(fam, 0) + 16
        return self.dma_counts[fam]

    def step(self, eng, fn, dma_waits=(), auto=True, inc=True, ewaits=()):
        waits = []
        if auto:
            for o in ENGS:
                if o != eng and self.counts[o] > self.seen[eng][o]:
                    waits.append((o, self.counts[o]))
                    self.seen[eng][o] = self.counts[o]
        for (o, cnt) in ewaits:
            if cnt > self.seen[eng][o]:
                waits.append((o, cnt))
                self.seen[eng][o] = cnt
        dw = []
        for (f, cnt) in dma_waits:
            if cnt > self.dma_seen[eng].get(f, 0):
                dw.append((f, cnt))
                self.dma_seen[eng][f] = cnt
        self.items.append((eng, waits, dw, fn, inc))
        if inc:
            self.counts[eng] += 1
        return self.counts[eng]


class Prog:
    def __init__(self, phases=99):
        self.phases = phases
        self.nc = bass.Bass("TRN2", target_bir_lowering=False)
        self.seq = Seq()
        self.es = ExitStack()
        self._declare()

    def _declare(self):
        nc = self.nc
        din = lambda n_, sh, dt=F32: nc.dram_tensor(n_, sh, dt, kind="ExternalInput").ap()
        self.inp = {}
        for n_, sh, dt in [
            ("x_t", [D, N], F32),
            ("l0_wq", [D, D], F32), ("l0_wk", [D, D], F32),
            ("l0_wv", [D, D], F32), ("l0_wo", [D, D], F32),
            ("l0_wg", [D, I8], F32), ("l0_wu", [D, I8], F32),
            ("l0_wd", [I8, D], F32),
            ("l1_wq", [D, D], F32), ("l1_wk", [D, G1 * 128], F32),
            ("l1_wv", [D, G1 * 128], F32), ("l1_wo", [D, D], F32),
            ("l1_agg", [D, D], F32), ("l1_bc", [D, D], F32),
            ("l1_gw", [D, D], F32),
            ("l1_d1", [D, I8], F32), ("l1_d2", [I8, D], F32),
            ("l1_d1b", [1, I8], F32), ("l1_d2b", [1, D], F32),
            ("gam", [128, 7 * DC], F32),
            ("aggb_c", [128, DC], F32), ("bcb_c", [128, DC], F32),
            ("c_ones", [128, 128], BF16), ("c_iden", [128, 128], BF16),
            ("c_had", [128, 128], BF16), ("c_mk", [128, 896], BF16),
            ("c_pos", [1, N], F32), ("c_j64", [128, 1], F32),
            ("c_neg1", [1, 512], F32), ("c_onesf", [128, 1], F32),
            ("c_onerf", [1, 128], F32),
        ]:
            self.inp[n_] = din(n_, sh, dt)
        self.y_t = nc.dram_tensor("y_t", [D, N], F32, kind="ExternalOutput").ap()
        self.trunc = self.phases < 99
        if self.trunc:
            self.dbg = nc.dram_tensor("dbg", [128, 2 * DC * N + 4608], BF16,
                                      kind="ExternalOutput").ap()
        self.s_rc = nc.dram_tensor("s_rc", [128, 4], F32, kind="Internal").ap()
        self.s_rr = nc.dram_tensor("s_rr", [1, 512], BF16, kind="Internal").ap()
        self.s_rf = nc.dram_tensor("s_rf", [1, 512], F32, kind="Internal").ap()
        self.s_h = nc.dram_tensor("s_h", [I8, N], BF16,
                                  kind="ExternalOutput" if self.trunc
                                  else "Internal").ap()
        self.s_wd = nc.dram_tensor("s_wd", [I8, D], BF16, kind="Internal").ap()
        self.s_d1 = nc.dram_tensor("s_d1", [D, I8], BF16, kind="Internal").ap()
        self.s_d2 = nc.dram_tensor("s_d2", [I8, D], BF16, kind="Internal").ap()

        es = self.es
        sb = lambda n_, sh, dt=BF16: es.enter_context(nc.sbuf_tensor(n_, sh, dt))
        ps = lambda n_, sh, dt=F32: es.enter_context(nc.psum_tensor(n_, sh, dt))
        self.X = sb("X", [128, DC * N])
        self.A = sb("A", [128, DC * N])
        self.HB = sb("HB", [128, 4096])
        self.B = sb("B", [128, 2048])
        self.C = sb("C", [128, 4096])
        self.WT = sb("WT", [128, 2048])
        self.W32 = sb("W32", [128, 512], F32)
        self.VF = sb("VF", [128, 2048], F32)
        self.MSW = sb("MSW", [128, 1024], F32)
        self.RBC = sb("RBC", [128, 2048])
        self.ROWF = sb("ROWF", [128, 1024], F32)
        self.RB = sb("RB", [128, 4608])
        self.SC = sb("SC", [128, 64], F32)
        self.SCB = sb("SCB", [128, 96])
        self.THC = sb("THC", [128, DC], F32)
        self.BCC = sb("BCC", [128, DC], F32)
        self.ACC = sb("ACC", [128, 8], F32)
        self.ZT = sb("ZT", [128, 128])
        self.Z512 = sb("Z512", [128, 512], F32)
        self.CONES = sb("CONES", [128, 128])
        self.CIDEN = sb("CIDEN", [128, 128])
        self.CHAD = sb("CHAD", [128, 128])
        self.CMK = sb("CMK", [128, 896])
        self.POS = sb("POS", [128, N], F32)
        self.J64 = sb("J64", [128, 1], F32)
        self.NEG1 = sb("NEG1", [128, 512], F32)
        self.ONE1F = sb("ONE1F", [128, 1], F32)
        self.ONERF = sb("ONERF", [128, 128], F32)
        self.GAM = sb("GAM", [128, 7 * DC], F32)
        self.AGGBC = sb("AGGBC", [128, DC], F32)
        self.BCBC = sb("BCBC", [128, DC], F32)
        self.EPSC = sb("EPSC", [128, 2], F32)

        self.P0 = ps("P0", [128, 512])
        self.P1 = ps("P1", [128, 512])
        self.P2 = ps("P2", [128, 512])
        self.P3 = ps("P3", [128, 512])
        self.P4 = ps("P4", [128, 512])
        self.P5 = ps("P5", [128, 512])
        self.PR = ps("PR", [128, 512])
        self.PTB = ps("PTB", [128, 512], BF16)
        self.PB = {0: self.P0, 1: self.P2, 2: self.P3, 3: self.P5}

    def xch(self, ch):
        return self.X[:, ch * N:(ch + 1) * N]

    def ach(self, ch):
        return self.A[:, ch * N:(ch + 1) * N]

    def aslot512(self, s):
        return self.A[:, s * 512:(s + 1) * 512]

    def vf(self, i):
        return self.VF[:, i * 512:(i + 1) * 512]

    def bsl(self, i):
        return self.B[:, i * 512:(i + 1) * 512]

    def pe(self, fn, **kw):
        self.seq.step("tensor", fn, **kw)

    def ve(self, fn, **kw):
        self.seq.step("vector", fn, **kw)

    def se(self, fn, **kw):
        self.seq.step("scalar", fn, **kw)

    def load(self, fam, dst, src, ewaits=None, dwaits=(), slow=False):
        tgt = self.seq.dma_tick(fam)
        def fn(h, s, dst=dst, src=src, fam=fam, slow=slow):
            return h.dma_start(out=dst, in_=src,
                               allow_slow_non_contiguous=slow
                               ).then_inc(s.dsem[fam], 16)
        if ewaits is None:
            self.seq.step("sync", fn, auto=True, inc=False, dma_waits=dwaits)
        else:
            self.seq.step("sync", fn, auto=False, inc=False, ewaits=ewaits,
                          dma_waits=dwaits)
        return tgt

    def col_to_row(self, src_cols, dst_row):
        prc = self.seq.dma_counts.get("rcr", 0)
        t1 = self.store("rcw", self.s_rc, src_cols,
                        dwaits=[("rcr", prc)] if prc else ())
        return self.load("rcr", dst_row.rearrange("a (c o) -> a c o", o=128),
                         self.s_rc.rearrange("p c -> () c p"),
                         dwaits=[("rcw", t1)], slow=True)

    def store(self, fam, dst, src, dwaits=()):
        tgt = self.seq.dma_tick(fam)
        def fn(h, s, dst=dst, src=src, fam=fam):
            return h.dma_start(out=dst, in_=src).then_inc(s.dsem[fam], 16)
        self.seq.step("sync", fn, auto=True, inc=False, dma_waits=dwaits)
        return tgt

    def mm(self, out, lhsT, rhs, start=True, stop=True):
        def fn(h, s, out=out, lhsT=lhsT, rhs=rhs, a=start, b=stop):
            return h.matmul(out, lhsT, rhs, start=a, stop=b)
        self.pe(fn)

    def ph_init(self):
        sq = self.seq
        for dst, src in [(self.CONES[:], self.inp["c_ones"]),
                         (self.CIDEN[:], self.inp["c_iden"]),
                         (self.CHAD[:], self.inp["c_had"]),
                         (self.CMK[:], self.inp["c_mk"]),
                         (self.POS[0:1, :], self.inp["c_pos"]),
                         (self.J64[:], self.inp["c_j64"]),
                         (self.NEG1[0:1, :], self.inp["c_neg1"]),
                         (self.ONE1F[:], self.inp["c_onesf"]),
                         (self.ONERF[0:1, :], self.inp["c_onerf"]),
                         (self.GAM[:], self.inp["gam"]),
                         (self.AGGBC[:], self.inp["aggb_c"]),
                         (self.BCBC[:], self.inp["bcb_c"])]:
            self.load("cst", dst, src)
        cst = sq.dma_counts["cst"]
        def f0(h, s):
            h.memset(self.ZT[:], 0.0)
            h.memset(self.Z512[:], 0.0)
            h.memset(self.EPSC[:, 0:1], 1e-5)
            return h.memset(self.EPSC[:, 1:2], 1e-6)
        self.ve(f0, dma_waits=[("cst", cst)])
        for ch in range(DC):
            for j in range(TSB):
                tgt = self.load("xin", self.W32[:, 0:512],
                                self.inp["x_t"][ch * 128:(ch + 1) * 128,
                                                j * 512:(j + 1) * 512],
                                ewaits=[("vector", sq.counts["vector"])])
                def fc(h, s, ch=ch, j=j):
                    return h.tensor_copy(self.xch(ch)[:, j * 512:(j + 1) * 512],
                                         self.W32[:, 0:512])
                self.ve(fc, dma_waits=[("xin", tgt)])
        def fj1(h, s):
            return h.tensor_scalar(self.ROWF[0:1, 128:256], self.POS[0:1, 0:128],
                                   64.0, 64.0, ALU.is_ge, ALU.mult)
        self.ve(fj1)
        def fj2(h, s):
            return h.tensor_tensor(self.ROWF[0:1, 256:384], self.POS[0:1, 0:128],
                                   self.ROWF[0:1, 128:256], ALU.subtract)
        self.ve(fj2)
        def fth(h, s):
            return h.activation(self.ROWF[0:1, 0:128], self.ROWF[0:1, 256:384],
                                AF.Exp, scale=-LN1E4 / 64.0)
        self.se(fth)
        def fsg1(h, s):
            return h.tensor_scalar(self.ROWF[0:1, 128:256], self.POS[0:1, 0:128],
                                   64.0, 2.0, ALU.is_ge, ALU.mult)
        self.ve(fsg1)
        def fsg2(h, s):
            return h.tensor_scalar(self.ROWF[0:1, 256:384],
                                   self.ROWF[0:1, 128:256], -1.0, None, ALU.add)
        self.ve(fsg2)
        def fsg3(h, s):
            return h.tensor_tensor(self.ROWF[0:1, 0:128], self.ROWF[0:1, 0:128],
                                   self.ROWF[0:1, 256:384], ALU.mult)
        self.ve(fsg3)
        for j in range(TSB):
            tsl = slice(j * 512, (j + 1) * 512)
            self.mm(self.P4[:, 0:512], self.ROWF[0:1, 0:128], self.POS[0:1, tsl])
            for trig, off in ((0, 0.0), (1, PI / 2)):
                def fu(h, s, off=off):
                    return h.tensor_scalar(self.vf(0), self.P4[:, 0:512],
                                           1.0 / TWO_PI,
                                           512.5 + off / TWO_PI,
                                           ALU.mult, ALU.add)
                self.ve(fu)
                def fu2(h, s):
                    return h.tensor_copy(self.vf(1), self.vf(0))
                self.ve(fu2)
                dcur, dnxt = 1, 2
                for k in range(9, -1, -1):
                    def fb(h, s, k=k, dc=dcur):
                        return h.tensor_scalar(self.vf(3), self.vf(dc),
                                               float(2 ** k), float(2 ** k),
                                               ALU.is_ge, ALU.mult)
                    self.ve(fb)
                    def fd(h, s, dc=dcur, dn=dnxt):
                        return h.tensor_tensor(self.vf(dn), self.vf(dc),
                                               self.vf(3), ALU.subtract)
                    self.ve(fd)
                    dcur, dnxt = dnxt, dcur
                def fk(h, s, dc=dcur):
                    return h.tensor_tensor(self.vf(3), self.vf(0), self.vf(dc),
                                           ALU.subtract)
                self.ve(fk)
                def fr(h, s):
                    return h.scalar_tensor_tensor(self.vf(1), self.vf(3),
                                                  -TWO_PI, self.P4[:, 0:512],
                                                  ALU.mult, ALU.add)
                self.ve(fr)
                def fr2(h, s, off=off):
                    return h.tensor_scalar(self.vf(2), self.vf(1),
                                           off + 1024.0 * PI, -PI,
                                           ALU.add, ALU.max)
                self.ve(fr2)
                def fr3(h, s):
                    return h.tensor_scalar(self.vf(3), self.vf(2), PI, None,
                                           ALU.min)
                self.ve(fr3)
                def fs(h, s, trig=trig, tsl=tsl):
                    return h.activation(
                        self.C[:, trig * N + tsl.start: trig * N + tsl.stop],
                        self.vf(3), AF.Sin)
                self.se(fs)

    def ph_stats(self, eps_col, src=None):
        src = src or self.xch
        for j in range(TSB):
            tsl = slice(j * 512, (j + 1) * 512)
            def fsum(h, s, tsl=tsl, src=src):
                for ch in range(DC):
                    r = h.matmul(self.PR[0:1, 0:512], self.CONES[:, 0:1],
                                 src(ch)[:, tsl], start=(ch == 0),
                                 stop=(ch == DC - 1))
                return r
            self.pe(fsum)
            for ch in range(DC):
                def fsq(h, s, ch=ch, tsl=tsl, src=src):
                    return h.activation(self.bsl(3), src(ch)[:, tsl],
                                        AF.Square)
                self.se(fsq)
                def fss(h, s, ch=ch):
                    return h.matmul(self.PR[32:33, 0:512], self.CONES[:, 0:1],
                                    self.bsl(3), start=(ch == 0),
                                    stop=(ch == DC - 1))
                self.pe(fss)
            def fnm(h, s, tsl=tsl):
                return h.tensor_scalar(self.RB[0:1, tsl], self.PR[0:1, 0:512],
                                       -1.0 / D, None, ALU.mult)
            self.ve(fnm)
            def fmu(h, s):
                return h.tensor_scalar(self.ROWF[0:1, 0:512],
                                       self.PR[0:1, 0:512], 1.0 / D, None,
                                       ALU.mult)
            self.ve(fmu)
            def fm2(h, s):
                return h.tensor_tensor(self.ROWF[0:1, 512:1024],
                                       self.ROWF[0:1, 0:512],
                                       self.ROWF[0:1, 0:512], ALU.mult)
            self.ve(fm2)
            def fvar(h, s):
                return h.scalar_tensor_tensor(self.ROWF[0:1, 0:512],
                                              self.PR[32:33, 0:512], 1.0 / D,
                                              self.ROWF[0:1, 512:1024],
                                              ALU.mult, ALU.subtract)
            self.ve(fvar)
            epsv = 1e-5 if eps_col == 0 else 1e-6
            def fre(h, s, epsv=epsv):
                return h.tensor_scalar(self.ROWF[0:1, 512:1024],
                                       self.ROWF[0:1, 0:512], epsv, None,
                                       ALU.add)
            self.ve(fre)
            def frr(h, s):
                return h.reciprocal(self.ROWF[0:1, 0:512],
                                    self.ROWF[0:1, 512:1024])
            self.ve(frr)
            def fr(h, s, tsl=tsl):
                return h.activation(self.RB[0:1, N + tsl.start:N + tsl.stop],
                                    self.ROWF[0:1, 0:512], AF.Sqrt)
            self.se(fr)
            def fbc(h, s, tsl=tsl):
                return h.matmul(self.P4[:, 0:512], self.CONES[0:1, :],
                                self.RB[0:1, N + tsl.start:N + tsl.stop],
                                start=True, stop=True)
            self.pe(fbc)
            def fcp(h, s, tsl=tsl):
                return h.activation(self.RBC[:, tsl], self.P4[:, 0:512], AF.Copy)
            self.se(fcp)

    def load_wtile(self, fam, src, c0, scale=None, wsum=False, nch=DC):
        sq = self.seq
        rearr = src[:, c0:c0 + 128].rearrange("(c p) o -> p c o", p=128)
        dst3 = self.W32[:, 0:512].rearrange("p (c o) -> p c o", o=128)
        for q in range(nch // 4):
            sl = rearr[:, q * 4:(q + 1) * 4, :]
            tgt = self.load(fam, dst3, sl,
                            ewaits=[("vector", sq.counts["vector"])])
            if scale is None:
                def fc(h, s, q=q):
                    return h.tensor_copy(self.WT[:, q * 512:(q + 1) * 512],
                                         self.W32[:, 0:512])
            else:
                def fc(h, s, q=q, scale=scale):
                    return h.tensor_scalar(self.WT[:, q * 512:(q + 1) * 512],
                                           self.W32[:, 0:512], scale, None,
                                           ALU.mult)
            self.ve(fc, dma_waits=[(fam, tgt)])
        if wsum:
            def fws(h, s, nch=nch):
                for ch in range(nch):
                    r = h.matmul(self.PR[64:65, 0:128], self.CONES[:, 0:1],
                                 self.WT[:, ch * 128:(ch + 1) * 128],
                                 start=(ch == 0), stop=(ch == nch - 1))
                return r
            self.pe(fws)
            def fwc(h, s):
                return h.tensor_copy(self.RB[0:1, 2 * N:2 * N + 128],
                                     self.PR[64:65, 0:128])
            self.ve(fwc)

    def proj_psum(self, pbank, tsl, nch=DC, fold=True, src=None):
        src = src or self.xch
        def fp(h, s, tsl=tsl, nch=nch, fold=fold, pbank=pbank, src=src):
            for ch in range(nch):
                r = h.matmul(pbank[:, 0:512], self.WT[:, ch * 128:(ch + 1) * 128],
                             src(ch)[:, tsl], start=(ch == 0),
                             stop=(ch == nch - 1 and not fold))
            if fold:
                r = h.matmul(pbank[:, 0:512], self.RB[0:1, 2 * N:2 * N + 128],
                             self.RB[0:1, tsl], start=False, stop=True)
            return r
        self.pe(fp)

    def evac_rope(self, dst, tsl):
        def fall(h, s, dst=dst, tsl=tsl):
            h.tensor_tensor(self.vf(0), self.P0[:, 0:512], self.RBC[:, tsl],
                            ALU.mult)
            h.tensor_copy(self.vf(1)[0:64, :], self.vf(0)[64:128, :])
            h.tensor_copy(self.vf(1)[64:128, :], self.vf(0)[0:64, :])
            h.tensor_tensor(self.vf(2), self.vf(0),
                            self.C[:, tsl.start + N:tsl.stop + N], ALU.mult)
            h.tensor_tensor(self.vf(3), self.vf(1),
                            self.C[:, tsl.start:tsl.stop], ALU.mult)
            return h.tensor_tensor(dst, self.vf(2), self.vf(3), ALU.add)
        self.ve(fall)

    def ph_l0_attn(self):
        for hd in range(HEADS):
            self.load_wtile("wp", self.inp["l0_wk"], hd * 128, wsum=True)
            for j in range(TSB):
                tsl = slice(j * 512, (j + 1) * 512)
                self.proj_psum(self.P0, tsl)
                self.evac_rope(self.HB[:, tsl], tsl)
            self.load_wtile("wp", self.inp["l0_wv"], hd * 128, wsum=True)
            for j in range(TSB):
                tsl = slice(j * 512, (j + 1) * 512)
                self.proj_psum(self.P0, tsl)
                def fv(h, s, tsl=tsl):
                    return h.tensor_tensor(self.bsl(3), self.P0[:, 0:512],
                                           self.RBC[:, tsl], ALU.mult)
                self.ve(fv)
                def ftr(h, s):
                    for i in range(4):
                        r = h.transpose(self.PTB[:, i * 128:(i + 1) * 128],
                                        self.B[:, 1536 + i * 128:1536 + (i + 1) * 128],
                                        self.CIDEN[:])
                    return r
                self.pe(ftr)
                def fvc(h, s, j=j):
                    return h.activation(self.HB[:, 2048 + j * 512:2048 + (j + 1) * 512],
                                        self.PTB[:, 0:512], AF.Copy)
                self.se(fvc)
            self.load_wtile("wp", self.inp["l0_wq"], hd * 128, wsum=True)
            for qb in range(TSB):
                qsl = slice(qb * 512, (qb + 1) * 512)
                self.proj_psum(self.P0, qsl)
                self.evac_rope(self.bsl(0), qsl)
                nkt = (qb + 1) * 4
                for kt in range(nkt):
                    uu = kt - 4 * qb
                    def fsc(h, s, kt=kt):
                        return h.matmul(self.P2[:, 0:512],
                                        self.HB[:, kt * 128:(kt + 1) * 128],
                                        self.bsl(0), start=True, stop=True)
                    self.pe(fsc)
                    def fs2(h, s):
                        return h.activation(self.bsl(1), self.P2[:, 0:512],
                                            AF.Square)
                    self.se(fs2)
                    def fs4(h, s):
                        return h.activation(self.bsl(2), self.bsl(1), AF.Square)
                    self.se(fs4)
                    cur = 2
                    if uu >= 0:
                        mo = 384 - 128 * uu
                        def fmk(h, s, mo=mo):
                            return h.tensor_tensor(self.bsl(1), self.bsl(2),
                                                   self.CMK[:, mo:mo + 512],
                                                   ALU.mult)
                        self.ve(fmk)
                        cur = 1
                    def fden(h, s, cur=cur, kt=kt, nkt=nkt):
                        return h.matmul(self.PR[0:1, 0:512], self.CONES[:, 0:1],
                                        self.bsl(cur), start=(kt == 0),
                                        stop=(kt == nkt - 1))
                    self.pe(fden)
                    def fpv(h, s, cur=cur, kt=kt, nkt=nkt):
                        return h.matmul(self.P3[:, 0:512],
                                        self.HB[:, 2048 + kt * 128:2048 + (kt + 1) * 128],
                                        self.bsl(cur), start=(kt == 0),
                                        stop=(kt == nkt - 1))
                    self.pe(fpv)
                def fdn(h, s):
                    return h.tensor_scalar(self.ROWF[0:1, 0:512],
                                           self.PR[0:1, 0:512], 1e-6, None,
                                           ALU.max)
                self.ve(fdn)
                def frc(h, s):
                    return h.reciprocal(self.ROWF[0:1, 512:1024],
                                        self.ROWF[0:1, 0:512])
                self.ve(frc)
                def fb(h, s):
                    return h.matmul(self.P4[:, 0:512], self.ONERF[0:1, :],
                                    self.ROWF[0:1, 512:1024], start=True,
                                    stop=True)
                self.pe(fb)
                def fcp(h, s):
                    return h.activation(self.vf(0), self.P4[:, 0:512], AF.Copy)
                self.se(fcp)
                def fo(h, s, hd=hd, qsl=qsl):
                    return h.tensor_tensor(self.ach(hd)[:, qsl],
                                           self.P3[:, 0:512], self.vf(0),
                                           ALU.mult)
                self.ve(fo)
        self._wo_residual(self.inp["l0_wo"])

    def _wo_residual(self, wsrc):
        for oc in range(DC):
            self.load_wtile("wo", wsrc, oc * 128)
            for j in range(TSB):
                tsl = slice(j * 512, (j + 1) * 512)
                self.proj_psum(self.P0, tsl, fold=False, src=self.ach)
                def fa(h, s, oc=oc, tsl=tsl):
                    return h.tensor_tensor(self.xch(oc)[:, tsl],
                                           self.xch(oc)[:, tsl],
                                           self.P0[:, 0:512], ALU.add)
                self.ve(fa)

    def ph_prepass(self):
        sq = self.seq
        specs = [("wg", self.inp["l0_wg"], D, I8, 0),
                 ("wu", self.inp["l0_wu"], D, I8, 2),
                 ("wd", self.inp["l0_wd"], I8, D, 4)]
        for nm, src, R, Cc, sci in specs:
            def fz(h, s, sci=sci):
                return h.memset(self.ACC[:, sci:sci + 1], 0.0)
            self.ve(fz)
            ntile = (R // 128) * (Cc // 512)
            ti = 0
            for r in range(R // 128):
                for cc in range(0, Cc, 512):
                    tgt = self.load("ffA", self.W32[:, 0:512],
                                    src[r * 128:(r + 1) * 128, cc:cc + 512],
                                    ewaits=[("scalar", sq.counts["scalar"]),
                                            ("vector", sq.counts["vector"])])
                    def fab(h, s, sci=sci):
                        return h.activation(self.vf(0), self.W32[:, 0:512],
                                            AF.Abs,
                                            accum_out=self.ACC[:, sci + 1:sci + 2])
                    self.se(fab, dma_waits=[("ffA", tgt)])
                    def fac(h, s, sci=sci):
                        return h.tensor_tensor(self.ACC[:, sci:sci + 1],
                                               self.ACC[:, sci:sci + 1],
                                               self.ACC[:, sci + 1:sci + 2],
                                               ALU.add)
                    self.ve(fac)
                    ti += 1
            def ft(h, s, sci=sci):
                return h.matmul(self.PR[0:1, 0:1], self.ONE1F[:],
                                self.ACC[:, sci:sci + 1], start=True, stop=True)
            self.pe(ft)
            def fsb(h, s, RC=float(R) * float(Cc)):
                return h.tensor_scalar(self.ROWF[0:1, 0:1], self.PR[0:1, 0:1],
                                       1.0 / RC, 1e-5, ALU.mult, ALU.max)
            self.ve(fsb)
            def fb1(h, s):
                return h.matmul(self.P4[:, 0:1], self.ONERF[0:1, :],
                                self.ROWF[0:1, 0:1], start=True, stop=True)
            self.pe(fb1)
            def fb2(h, s, sci=sci):
                return h.activation(self.SC[:, sci:sci + 1], self.P4[:, 0:1],
                                    AF.Copy)
            self.se(fb2)
            def fiv(h, s, sci=sci):
                return h.reciprocal(self.SC[:, sci + 1:sci + 2],
                                    self.SC[:, sci:sci + 1])
            self.ve(fiv)
        sslot = {}
        si = 0
        for r in range(I8 // 128):
            for cc in range(0, D, 512):
                tgt = self.load("ffB", self.W32[:, 0:512],
                                self.inp["l0_wd"][r * 128:(r + 1) * 128,
                                                  cc:cc + 512],
                                ewaits=[("vector", sq.counts["vector"])])
                bo = 512 * (si % 2)
                dw = [("ffB", tgt)]
                if (si % 2) in sslot:
                    dw.append(("scr", sslot[si % 2]))
                def fq(h, s, bo=bo):
                    h.scalar_tensor_tensor(self.vf(0), self.W32[:, 0:512],
                                           self.SC[:, 5:6], self.Z512[:],
                                           ALU.mult, ALU.add)
                    h.tensor_scalar(self.vf(1), self.vf(0), 0.5, 1.0,
                                    ALU.is_ge, ALU.mult)
                    h.tensor_scalar(self.vf(2), self.vf(0), -0.5, 1.0,
                                    ALU.is_lt, ALU.mult)
                    return h.tensor_tensor(self.B[:, bo:bo + 512], self.vf(1),
                                           self.vf(2), ALU.subtract)
                self.ve(fq, dma_waits=dw)
                sslot[si % 2] = self.store(
                    "scr", self.s_wd[r * 128:(r + 1) * 128, cc:cc + 512],
                    self.B[:, bo:bo + 512])
                si += 1
        for src, dst, R, Cc in [(self.inp["l1_d1"], self.s_d1, D, I8),
                                (self.inp["l1_d2"], self.s_d2, I8, D)]:
            for r in range(R // 128):
                for cc in range(0, Cc, 512):
                    tgt = self.load("ffA", self.W32[:, 0:512],
                                    src[r * 128:(r + 1) * 128, cc:cc + 512],
                                    ewaits=[("vector", sq.counts["vector"])])
                    bo = 512 * (si % 2)
                    dw = [("ffA", tgt)]
                    if (si % 2) in sslot:
                        dw.append(("scr", sslot[si % 2]))
                    def fc(h, s, bo=bo):
                        return h.tensor_copy(self.B[:, bo:bo + 512],
                                             self.W32[:, 0:512])
                    self.ve(fc, dma_waits=dw)
                    sslot[si % 2] = self.store(
                        "scr", dst[r * 128:(r + 1) * 128, cc:cc + 512],
                        self.B[:, bo:bo + 512])
                    si += 1

    def round_tile(self, dst, ya=0, nbits=8):
        def fall(h, s, dst=dst, ya=ya, nbits=nbits):
            h.tensor_scalar(self.vf(1), self.vf(ya), 0.5, None, ALU.add)
            h.tensor_scalar(self.vf(2), self.vf(ya), 128.5, None, ALU.add)
            dcur, dnxt = 2, 3
            for k in range(nbits - 1, -1, -1):
                h.tensor_scalar(self.vf(ya), self.vf(dcur), float(2 ** k),
                                float(2 ** k), ALU.is_ge, ALU.mult)
                h.tensor_tensor(self.vf(dnxt), self.vf(dcur), self.vf(ya),
                                ALU.subtract)
                dcur, dnxt = dnxt, dcur
            return h.tensor_tensor(dst, self.vf(1), self.vf(dcur),
                                   ALU.subtract)
        self.ve(fall)

    def ph_l0_ffn_a(self):
        for j in range(TSB):
            tsl = slice(j * 512, (j + 1) * 512)
            def fnb(h, s, tsl=tsl):
                return h.matmul(self.P4[:, 0:512], self.CONES[0:1, :],
                                self.RB[0:1, tsl], start=True, stop=True)
            self.pe(fnb)
            def fnc(h, s):
                return h.activation(self.MSW[:, 0:512], self.P4[:, 0:512],
                                    AF.Copy)
            self.se(fnc)
            for ch in range(DC):
                def f1(h, s, ch=ch, tsl=tsl):
                    return h.tensor_tensor(self.vf(0), self.xch(ch)[:, tsl],
                                           self.MSW[:, 0:512], ALU.add)
                self.ve(f1)
                def f2(h, s, ch=ch, tsl=tsl):
                    return h.tensor_tensor(self.ach(ch)[:, tsl], self.vf(0),
                                           self.RBC[:, tsl], ALU.mult)
                self.ve(f2)
        for ch in range(DC):
            for j in range(TSB):
                tsl = slice(j * 512, (j + 1) * 512)
                def fm(h, s, ch=ch, tsl=tsl):
                    return h.matmul(self.P3[:, 0:512], self.CHAD[:],
                                    self.ach(ch)[:, tsl], start=True, stop=True)
                self.pe(fm)
                def fc(h, s, ch=ch, tsl=tsl):
                    return h.activation(self.ach(ch)[:, tsl], self.P3[:, 0:512],
                                        AF.Copy)
                self.se(fc)
        perm = list(range(DC))
        spare = DC
        slot = lambda p: (self.HB[:, 0:N] if p == DC else self.ach(p))
        for st in range(4):
            b = 1 << st
            plan = []
            for a0 in range(DC):
                if a0 & b:
                    continue
                plan.append((perm[a0], perm[a0 | b], spare))
                perm[a0], spare = spare, perm[a0]
            def fst(h, s, plan=tuple(plan)):
                for (pa, pb_, sp) in plan:
                    h.tensor_tensor(slot(sp), slot(pa), slot(pb_), ALU.add)
                    r = h.tensor_tensor(slot(pb_), slot(pa), slot(pb_),
                                        ALU.subtract)
                return r
            self.ve(fst)
        if DC in perm:
            chx = perm.index(DC)
            def fmv(h, s, sp=spare):
                return h.tensor_copy(slot(sp), slot(DC))
            self.ve(fmv)
            perm[chx] = spare
        self.ffn_perm = tuple(perm)
        for j in range(TSB):
            tsl = slice(j * 512, (j + 1) * 512)
            def ftree(h, s, tsl=tsl, pp=tuple(perm)):
                h.tensor_tensor(self.vf(0), slot(pp[0])[:, tsl],
                                slot(pp[0])[:, tsl], ALU.mult)
                for a0 in range(1, DC):
                    h.tensor_tensor(self.vf(1), slot(pp[a0])[:, tsl],
                                    slot(pp[a0])[:, tsl], ALU.mult)
                    r = h.tensor_tensor(self.vf(0), self.vf(0), self.vf(1),
                                        ALU.max)
                return r
            self.ve(ftree)
            self._colmax_to_minv(j, C2048)
        for j in range(TSB):
            tsl = slice(j * 512, (j + 1) * 512)
            def fcst(h, s, tsl=tsl):
                return h.tensor_copy(self.MSW[:, 0:512], self.C[:, tsl])
            self.ve(fcst)
            def frc(h, s):
                return h.reciprocal(self.MSW[:, 512:1024], self.MSW[:, 0:512])
            self.ve(frc)
            for a0 in range(DC):
                p = perm[a0]
                def fy(h, s, p=p, tsl=tsl):
                    return h.scalar_tensor_tensor(self.vf(0), slot(p)[:, tsl],
                                                  C2048, self.MSW[:, 512:1024],
                                                  ALU.mult, ALU.mult)
                self.ve(fy)
                self.round_tile(slot(p)[:, tsl])

    def _colmax_to_minv(self, j, cnorm):
        tsl = slice(j * 512, (j + 1) * 512)
        def fcb(h, s):
            return h.tensor_copy(self.bsl(3), self.vf(0))
        self.ve(fcb)
        def ftr(h, s):
            for i in range(4):
                r = h.transpose(self.PTB[:, i * 128:(i + 1) * 128],
                                self.B[:, 1536 + i * 128:1536 + (i + 1) * 128],
                                self.CIDEN[:])
            return r
        self.pe(ftr)
        def frd(h, s):
            for i in range(4):
                r = h.reduce_max(self.SC[:, 8 + i:9 + i],
                                 self.PTB[:, i * 128:(i + 1) * 128], axis=AX.X)
            return r
        self.ve(frd)
        def fsq(h, s):
            return h.activation(self.SC[:, 12:16], self.SC[:, 8:12], AF.Sqrt)
        self.se(fsq)
        def fmv(h, s, cnorm=cnorm):
            return h.tensor_scalar(self.SC[:, 16:20], self.SC[:, 12:16],
                                   cnorm / 127.0, 1e-5 / 127.0, ALU.mult,
                                   ALU.max)
        self.ve(fmv)
        tgt = self.col_to_row(self.SC[:, 16:20], self.ROWF[0:1, 0:512])
        def fbb(h, s, tgt=tgt):
            return h.matmul(self.P4[:, 0:512], self.ONERF[0:1, :],
                            self.ROWF[0:1, 0:512], start=True, stop=True)
        self.pe(fbb, dma_waits=[("rcr", tgt)])
        def fcc(h, s, tsl=tsl):
            return h.activation(self.C[:, tsl], self.P4[:, 0:512], AF.Copy)
        self.se(fcc)

    def ph_l0_ffn_b(self):
        sq = self.seq
        perm = self.ffn_perm
        xs = lambda ch: self.ach(perm[ch])
        stW = {}
        for oc in range(IC):
            for wi, (wsrc, sci) in enumerate([(self.inp["l0_wg"], 0),
                                              (self.inp["l0_wu"], 2)]):
                rearr = wsrc[:, oc * 128:(oc + 1) * 128].rearrange(
                    "(c p) o -> p c o", p=128)
                dst3 = self.W32[:, 0:512].rearrange("p (c o) -> p c o", o=128)
                for q in range(4):
                    sl = rearr[:, q * 4:(q + 1) * 4, :]
                    tgt = self.load("ffw", dst3, sl,
                                    ewaits=[("vector", sq.counts["vector"])])
                    def fq(h, s, q=q, sci=sci):
                        h.scalar_tensor_tensor(self.vf(0), self.W32[:, 0:512],
                                               self.SC[:, sci + 1:sci + 2],
                                               self.Z512[:], ALU.mult, ALU.add)
                        h.tensor_scalar(self.vf(1), self.vf(0), 0.5, 1.0,
                                        ALU.is_ge, ALU.mult)
                        h.tensor_scalar(self.vf(2), self.vf(0), -0.5, 1.0,
                                        ALU.is_lt, ALU.mult)
                        return h.tensor_tensor(self.WT[:, q * 512:(q + 1) * 512],
                                               self.vf(1), self.vf(2),
                                               ALU.subtract)
                    self.ve(fq, dma_waits=[("ffw", tgt)])
                for j in range(TSB):
                    tsl = slice(j * 512, (j + 1) * 512)
                    def fp(h, s, tsl=tsl, j=j):
                        for ch in range(DC):
                            r = h.matmul(self.PB[j][:, 0:512],
                                         self.WT[:, ch * 128:(ch + 1) * 128],
                                         xs(ch)[:, tsl], start=(ch == 0),
                                         stop=(ch == DC - 1))
                        return r
                    self.pe(fp)
                for j in range(TSB):
                    tsl = slice(j * 512, (j + 1) * 512)
                    if wi == 0:
                        def fe(h, s, j=j, tsl=tsl):
                            return h.tensor_tensor(self.vf(0),
                                                   self.PB[j][:, 0:512],
                                                   self.C[:, tsl], ALU.mult)
                        self.ve(fe)
                        def fsl(h, s, j=j):
                            return h.activation(self.HB[:, j * 512:(j + 1) * 512],
                                                self.vf(0), AF.Silu,
                                                scale=self.SC[:, 0:1])
                        self.se(fsl)
                    else:
                        def fe(h, s, j=j, tsl=tsl):
                            return h.tensor_tensor(self.vf(0),
                                                   self.PB[j][:, 0:512],
                                                   self.C[:, tsl], ALU.mult)
                        self.ve(fe)
                        slot = (oc * 4 + j) % 2
                        bofs = 512 * slot
                        def fh(h, s, j=j, bofs=bofs):
                            return h.scalar_tensor_tensor(
                                self.B[:, bofs:bofs + 512], self.vf(0),
                                self.SC[:, 2:3],
                                self.HB[:, j * 512:(j + 1) * 512],
                                ALU.mult, ALU.mult)
                        dw = []
                        prev = stW.get(slot)
                        if prev is not None:
                            dw.append(("hst", prev))
                        self.ve(fh, dma_waits=dw)
                        stW[slot] = self.store(
                            "hst", self.s_h[oc * 128:(oc + 1) * 128, tsl],
                            self.B[:, bofs:bofs + 512])

    def ph_l0_ffn_c(self):
        sq = self.seq
        if self.trunc and self.DIAG:
            hstc = sq.dma_counts.get("hst", 0)
            self.load("hpr", self.bsl(3), self.s_h[0:128, 0:512],
                      dwaits=[("hst", hstc)])
            hc = sq.dma_counts["hpr"]
            def fcp(h, s):
                return h.tensor_copy(self.bsl(2), self.bsl(3))
            self.ve(fcp, dma_waits=[("hpr", hc)])
            self.store("dbg", self.dbg[:, 2 * DC * N + 4096:2 * DC * N + 4608],
                       self.bsl(2))
        for tb in range(TSB):
            tsl = slice(tb * 512, (tb + 1) * 512)
            hstc = sq.dma_counts.get("hst", 0)
            for ch in range(IC):
                self.load("hld", self.aslot512(ch),
                          self.s_h[ch * 128:(ch + 1) * 128, tsl],
                          dwaits=([("hst", hstc)] if ch == 0 else ()))
            hcnt = sq.dma_counts["hld"]
            for ch in range(IC):
                dw = [("hld", hcnt)] if ch == 0 else []
                def fm(h, s, ch=ch):
                    return h.matmul(self.P3[:, 0:512], self.CHAD[:],
                                    self.aslot512(ch), start=True, stop=True)
                self.pe(fm, dma_waits=dw)
                def fc(h, s, ch=ch):
                    return h.activation(self.aslot512(ch), self.P3[:, 0:512],
                                        AF.Copy)
                self.se(fc)
            perm = list(range(IC))
            spare = IC
            slot = lambda p: (self.HB[:, 0:512] if p == IC else self.aslot512(p))
            for st in range(6):
                b = 1 << st
                plan = []
                for a0 in range(IC):
                    if a0 & b:
                        continue
                    plan.append((perm[a0], perm[a0 | b], spare))
                    perm[a0], spare = spare, perm[a0]
                def fst(h, s, plan=tuple(plan)):
                    for (pa, pb_, sp) in plan:
                        h.tensor_tensor(slot(sp), slot(pa), slot(pb_), ALU.add)
                        r = h.tensor_tensor(slot(pb_), slot(pa), slot(pb_),
                                            ALU.subtract)
                    return r
                self.ve(fst)
            def ftree(h, s, pp=tuple(perm)):
                h.tensor_tensor(self.vf(0), slot(pp[0]), slot(pp[0]), ALU.mult)
                for a0 in range(1, IC):
                    h.tensor_tensor(self.vf(1), slot(pp[a0]), slot(pp[a0]),
                                    ALU.mult)
                    r = h.tensor_tensor(self.vf(0), self.vf(0), self.vf(1),
                                        ALU.max)
                return r
            self.ve(ftree)
            self._colmax_to_minv8()
            def frc(h, s):
                return h.reciprocal(self.MSW[:, 512:1024], self.MSW[:, 0:512])
            self.ve(frc)
            for a0 in range(IC):
                p = perm[a0]
                def fy(h, s, p=p):
                    return h.scalar_tensor_tensor(self.vf(0), slot(p), C8192,
                                                  self.MSW[:, 512:1024],
                                                  ALU.mult, ALU.mult)
                self.ve(fy)
                self.round_tile(slot(p))
            def fmd(h, s):
                return h.scalar_tensor_tensor(self.MSW[:, 512:1024],
                                              self.MSW[:, 0:512],
                                              self.SC[:, 4:5], self.Z512[:],
                                              ALU.mult, ALU.add)
            self.ve(fmd)
            for oc in range(DC):
                rearr = self.s_wd[:, oc * 128:(oc + 1) * 128].rearrange(
                    "(c p) o -> p c o", p=128)
                wt3 = self.WT[:, 0:2048].rearrange("p (c o) -> p c o", o=128)
                scrc = sq.dma_counts.get("scr", 0)
                for qtr in range(4):
                    sl = rearr[:, qtr * 16:(qtr + 1) * 16, :]
                    tgt = self.load("wbt", wt3, sl,
                                    ewaits=[("tensor", sq.counts["tensor"])],
                                    dwaits=[("scr", scrc)])
                    def fp(h, s, qtr=qtr, tgt=tgt, pp=tuple(perm)):
                        for ci in range(16):
                            ch = qtr * 16 + ci
                            r = h.matmul(self.P0[:, 0:512],
                                         self.WT[:, ci * 128:(ci + 1) * 128],
                                         slot(pp[ch]),
                                         start=(qtr == 0 and ci == 0),
                                         stop=(qtr == 3 and ci == 15))
                        return r
                    self.pe(fp, dma_waits=[("wbt", tgt)])
                def fe(h, s):
                    return h.tensor_tensor(self.vf(0), self.P0[:, 0:512],
                                           self.MSW[:, 512:1024], ALU.mult)
                self.ve(fe)
                def fa(h, s, oc=oc, tsl=tsl):
                    return h.tensor_tensor(self.xch(oc)[:, tsl],
                                           self.xch(oc)[:, tsl], self.vf(0),
                                           ALU.add)
                self.ve(fa)

    def _colmax_to_minv8(self):
        def fcb(h, s):
            return h.tensor_copy(self.bsl(3), self.vf(0))
        self.ve(fcb)
        def ftr(h, s):
            for i in range(4):
                r = h.transpose(self.PTB[:, i * 128:(i + 1) * 128],
                                self.B[:, 1536 + i * 128:1536 + (i + 1) * 128],
                                self.CIDEN[:])
            return r
        self.pe(ftr)
        def frd(h, s):
            for i in range(4):
                r = h.reduce_max(self.SC[:, 8 + i:9 + i],
                                 self.PTB[:, i * 128:(i + 1) * 128], axis=AX.X)
            return r
        self.ve(frd)
        def fsq(h, s):
            return h.activation(self.SC[:, 12:16], self.SC[:, 8:12], AF.Sqrt)
        self.se(fsq)
        def fmv(h, s):
            return h.tensor_scalar(self.SC[:, 16:20], self.SC[:, 12:16],
                                   C8192 / 127.0, 1e-5 / 127.0, ALU.mult,
                                   ALU.max)
        self.ve(fmv)
        tgt = self.col_to_row(self.SC[:, 16:20], self.ROWF[0:1, 0:512])
        def fbb(h, s, tgt=tgt):
            return h.matmul(self.P4[:, 0:512], self.ONERF[0:1, :],
                            self.ROWF[0:1, 0:512], start=True, stop=True)
        self.pe(fbb, dma_waits=[("rcr", tgt)])
        def fcc(h, s):
            return h.activation(self.MSW[:, 0:512], self.P4[:, 0:512], AF.Copy)
        self.se(fcc)

    def ph_l1_attn(self):
        for u in range(-3, 5):
            ci = u + 3
            if u >= 0:
                qs, ks = slice(128 * u, 128 * u + 512), slice(0, 128)
            else:
                qs, ks = slice(0, 512), slice(-128 * u, -128 * u + 128)
            def fdd(h, s, qs=qs, ks=ks):
                h.matmul(self.P4[:, 0:512], self.ONERF[0:1, :],
                         self.POS[0:1, qs], start=True, stop=False)
                return h.matmul(self.P4[:, 0:512], self.POS[0:1, ks],
                                self.NEG1[0:1, :], start=False, stop=True)
            self.pe(fdd)
            def frl(h, s):
                return h.tensor_scalar(self.vf(0), self.P4[:, 0:512], 0.0,
                                       None, ALU.max)
            self.ve(frl)
            def fsq(h, s, ci=ci):
                return h.activation(self.C[:, ci * 512:(ci + 1) * 512],
                                    self.vf(0), AF.Sqrt)
            self.se(fsq)
        for g in range(G1):
            self.load_wtile("wp", self.inp["l1_wk"], g * 128, wsum=True)
            for j in range(TSB):
                tsl = slice(j * 512, (j + 1) * 512)
                self.proj_psum(self.P0, tsl)
                def fk(h, s, tsl=tsl):
                    return h.tensor_tensor(self.HB[:, tsl], self.P0[:, 0:512],
                                           self.RBC[:, tsl], ALU.mult)
                self.ve(fk)
            self.load_wtile("wp", self.inp["l1_wv"], g * 128, wsum=True)
            for j in range(TSB):
                tsl = slice(j * 512, (j + 1) * 512)
                self.proj_psum(self.P0, tsl)
                def fv(h, s, tsl=tsl):
                    return h.tensor_tensor(self.bsl(3), self.P0[:, 0:512],
                                           self.RBC[:, tsl], ALU.mult)
                self.ve(fv)
                def ftr(h, s):
                    for i in range(4):
                        r = h.transpose(self.PTB[:, i * 128:(i + 1) * 128],
                                        self.B[:, 1536 + i * 128:1536 + (i + 1) * 128],
                                        self.CIDEN[:])
                    return r
                self.pe(ftr)
                def fvc(h, s, j=j):
                    return h.activation(self.HB[:, 2048 + j * 512:2048 + (j + 1) * 512],
                                        self.PTB[:, 0:512], AF.Copy)
                self.se(fvc)
            for rr in range(4):
                hd = g * 4 + rr
                slope = 2.0 ** (-(hd + 1) / 2.0)
                self.load_wtile("wp", self.inp["l1_wq"], hd * 128, scale=QSC,
                                wsum=True)
                for qb in range(TSB):
                    qsl = slice(qb * 512, (qb + 1) * 512)
                    self.proj_psum(self.P0, qsl)
                    def fq(h, s, qsl=qsl):
                        return h.tensor_tensor(self.bsl(0), self.P0[:, 0:512],
                                               self.RBC[:, qsl], ALU.mult)
                    self.ve(fq)
                    nkt = (qb + 1) * 4
                    for kt in range(nkt):
                        u = 4 * qb - kt
                        def fsc(h, s, kt=kt):
                            return h.matmul(self.P2[:, 0:512],
                                            self.HB[:, kt * 128:(kt + 1) * 128],
                                            self.bsl(0), start=True, stop=True)
                        self.pe(fsc)
                        if u <= 4:
                            sqd = self.C[:, (u + 3) * 512:(u + 4) * 512]
                        else:
                            qs = slice(128 * u, 128 * u + 512)
                            def fdd(h, s, qs=qs, kt=kt):
                                h.matmul(self.P4[:, 0:512], self.ONERF[0:1, :],
                                         self.POS[0:1, qs], start=True,
                                         stop=False)
                                return h.matmul(self.P4[:, 0:512],
                                                self.POS[0:1, 0:128],
                                                self.NEG1[0:1, :], start=False,
                                                stop=True)
                            self.pe(fdd)
                            def fsqg(h, s):
                                return h.activation(self.bsl(3),
                                                    self.P4[:, 0:512], AF.Sqrt)
                            self.se(fsqg)
                            sqd = self.bsl(3)
                        def fz(h, s, sqd=sqd, slope=slope):
                            return h.scalar_tensor_tensor(self.vf(1), sqd,
                                                          -slope,
                                                          self.P2[:, 0:512],
                                                          ALU.mult, ALU.add)
                        self.ve(fz)
                        if u <= 0:
                            mo = 384 - 128 * (-u)
                            def fm1(h, s, mo=mo):
                                return h.scalar_tensor_tensor(
                                    self.vf(0), self.CMK[:, mo:mo + 512], 1e5,
                                    self.vf(1), ALU.mult, ALU.add)
                            self.ve(fm1)
                            def fm2(h, s):
                                return h.tensor_scalar(self.vf(1), self.vf(0),
                                                       -1e5, None, ALU.add)
                            self.ve(fm2)
                        def fe(h, s):
                            return h.activation(self.bsl(1), self.vf(1), AF.Exp)
                        self.se(fe)
                        def fden(h, s, kt=kt, nkt=nkt):
                            return h.matmul(self.PR[0:1, 0:512],
                                            self.CONES[:, 0:1], self.bsl(1),
                                            start=(kt == 0), stop=(kt == nkt - 1))
                        self.pe(fden)
                        def fpv(h, s, kt=kt, nkt=nkt):
                            return h.matmul(self.P3[:, 0:512],
                                            self.HB[:, 2048 + kt * 128:2048 + (kt + 1) * 128],
                                            self.bsl(1), start=(kt == 0),
                                            stop=(kt == nkt - 1))
                        self.pe(fpv)
                    def frc(h, s):
                        return h.reciprocal(self.ROWF[0:1, 0:512],
                                            self.PR[0:1, 0:512])
                    self.ve(frc)
                    def fb(h, s):
                        return h.matmul(self.P4[:, 0:512], self.ONERF[0:1, :],
                                        self.ROWF[0:1, 0:512], start=True,
                                        stop=True)
                    self.pe(fb)
                    def fcp(h, s):
                        return h.activation(self.vf(0), self.P4[:, 0:512],
                                            AF.Copy)
                    self.se(fcp)
                    def fo(h, s, hd=hd, qsl=qsl):
                        return h.tensor_tensor(self.ach(hd)[:, qsl],
                                               self.P3[:, 0:512], self.vf(0),
                                               ALU.mult)
                    self.ve(fo)
        self._wo_residual(self.inp["l1_wo"])

    def ph_l1_ffn(self):
        sq = self.seq
        GI, GB = 6 * DC, 0
        gcol = lambda k: self.GAM[:, k * DC:(k + 1) * DC]
        self.ph_stats(eps_col=0)
        for oc in range(DC):
            self.load_wtile("gw", self.inp["l1_gw"], oc * 128, wsum=True)
            for j in range(TSB):
                tsl = slice(j * 512, (j + 1) * 512)
                self.proj_psum(self.P0, tsl, fold=True)
                def fgp(h, s, tsl=tsl):
                    return h.tensor_tensor(self.vf(0), self.P0[:, 0:512],
                                           self.RBC[:, tsl], ALU.mult)
                self.ve(fgp)
                def fg(h, s, oc=oc, tsl=tsl):
                    return h.activation(self.ach(oc)[:, tsl], self.vf(0),
                                        AF.Sigmoid,
                                        bias=gcol(6)[:, oc:oc + 1])
                self.se(fg)
        def fxl(h, s):
            src = self.X[:].rearrange("p (c n) -> p c n", n=N)[:, :, N - 1:N]
            return h.tensor_copy(self.SC[:, 32:48],
                                 src.rearrange("p c n -> p (c n)"))
        self.ve(fxl)
        def fzk(h, s):
            h.matmul(self.P4[:, 0:1], self.CONES[0:1, :],
                     self.RB[0:1, N - 1:N], start=True, stop=True)
            return h.matmul(self.P4[:, 1:2], self.CONES[0:1, :],
                            self.RB[0:1, 2 * N - 1:2 * N], start=True,
                            stop=True)
        self.pe(fzk)
        def fzc(h, s):
            return h.activation(self.SC[:, 28:30], self.P4[:, 0:2], AF.Copy)
        self.se(fzc)
        def fz1(h, s):
            return h.scalar_tensor_tensor(self.SC[:, 48:64], self.SC[:, 32:48],
                                          self.SC[:, 28:29],
                                          self.Z512[:, 0:16], ALU.add, ALU.add)
        self.ve(fz1)
        def fz2(h, s):
            return h.scalar_tensor_tensor(self.SC[:, 32:48], self.SC[:, 48:64],
                                          self.SC[:, 29:30],
                                          self.Z512[:, 0:16], ALU.mult,
                                          ALU.add)
        self.ve(fz2)
        def fs1r(h, s):
            return h.reduce_sum(self.SC[:, 25:26], self.SC[:, 32:48], axis=AX.X)
        self.ve(fs1r)
        def fs1(h, s):
            return h.matmul(self.PR[0:1, 0:1], self.ONE1F[:],
                            self.SC[:, 25:26], start=True, stop=True)
        self.pe(fs1)
        def fq2(h, s):
            h.tensor_tensor(self.SC[:, 48:64], self.SC[:, 32:48],
                            self.SC[:, 32:48], ALU.mult)
            return h.reduce_sum(self.SC[:, 24:25], self.SC[:, 48:64], axis=AX.X)
        self.ve(fq2)
        def fs2(h, s):
            return h.matmul(self.PR[32:33, 0:1], self.ONE1F[:],
                            self.SC[:, 24:25], start=True, stop=True)
        self.pe(fs2)
        self._vec_ln_cols(self.SC[:, 32:48], self.SCB[:, 16:32],
                          gcol(0), gcol(1), eps_col=1,
                          dsum=self.PR[0:1, 0:1], dss=self.PR[32:33, 0:1])
        self._colproj(self.inp["l1_agg"], self.SCB[:, 16:32], self.THC,
                      self.AGGBC, "agg")
        if self.trunc:
            def fdg(h, s):
                h.tensor_copy(self.B[:, 32:48], self.THC[:])
                h.tensor_copy(self.B[:, 48:64], self.SCB[:, 16:32])
                h.tensor_copy(self.B[:, 64:68], self.SC[:, 20:24])
                return h.tensor_copy(self.B[:, 68:84], self.SC[:, 32:48])
            self.ve(fdg)
        for step in range(4):
            def fq2b(h, s):
                h.tensor_tensor(self.SC[:, 48:64], self.THC[:],
                                self.THC[:], ALU.mult)
                h.reduce_sum(self.SC[:, 24:25], self.SC[:, 48:64], axis=AX.X)
                return h.reduce_sum(self.SC[:, 25:26], self.THC[:], axis=AX.X)
            self.ve(fq2b)
            def fs2b(h, s):
                h.matmul(self.PR[0:1, 0:1], self.ONE1F[:], self.SC[:, 25:26],
                         start=True, stop=True)
                return h.matmul(self.PR[32:33, 0:1], self.ONE1F[:],
                                self.SC[:, 24:25], start=True, stop=True)
            self.pe(fs2b)
            self._vec_ln_cols(self.THC[:], self.SCB[:, 0:16], gcol(2), gcol(3),
                              eps_col=1, dsum=self.PR[0:1, 0:1],
                              dss=self.PR[32:33, 0:1])
            for os in range(16):
                osl = slice(os * 512, (os + 1) * 512)
                scrc = sq.dma_counts.get("scr", 0)
                for ch in range(DC):
                    tgt = self.load("dld", self.WT[:, (ch % 4) * 512:(ch % 4 + 1) * 512],
                                    self.s_d1[ch * 128:(ch + 1) * 128, osl],
                                    ewaits=[("tensor", sq.counts["tensor"])],
                                    dwaits=[("scr", scrc)])
                    def fp(h, s, ch=ch, tgt=tgt):
                        return h.matmul(self.PR[0:1, 0:512],
                                        self.SCB[:, ch:ch + 1],
                                        self.WT[:, (ch % 4) * 512:(ch % 4 + 1) * 512],
                                        start=(ch == 0), stop=(ch == DC - 1))
                    self.pe(fp, dma_waits=[("dld", tgt)])
                tb = self.load("dlb", self.ROWF[0:1, 512:1024],
                               self.inp["l1_d1b"][0:1, osl],
                               ewaits=[("vector", sq.counts["vector"])])
                def fba(h, s):
                    return h.tensor_tensor(self.ROWF[0:1, 0:512],
                                           self.PR[0:1, 0:512],
                                           self.ROWF[0:1, 512:1024], ALU.add)
                self.ve(fba, dma_waits=[("dlb", tb)])
                def fgl(h, s):
                    return h.activation(self.RB[0:1, 4096:4608],
                                        self.ROWF[0:1, 0:512], AF.Gelu)
                self.se(fgl)
                pms = sq.dma_counts.get("msc", 0)
                t1 = self.store("rcw", self.s_rr, self.RB[0:1, 4096:4608],
                                dwaits=[("msc", pms)] if pms else ())
                self.load("msc",
                          self.SCB[:, 32 + os * 4:32 + (os + 1) * 4],
                          self.s_rr.rearrange("a (c p) -> p c", p=128),
                          dwaits=[("rcw", t1)], slow=True)
            mscc = sq.dma_counts.get("msc", 0)
            def fbar(h, s):
                return h.memset(self.ROWF[0:1, 1020:1021], 0.0)
            self.ve(fbar, dma_waits=[("msc", mscc)])
            for os2 in range(4):
                osl = slice(os2 * 512, (os2 + 1) * 512)
                scrc = sq.dma_counts.get("scr", 0)
                for ch in range(IC):
                    tgt = self.load("dld", self.WT[:, (ch % 4) * 512:(ch % 4 + 1) * 512],
                                    self.s_d2[ch * 128:(ch + 1) * 128, osl],
                                    ewaits=[("tensor", sq.counts["tensor"])],
                                    dwaits=[("scr", scrc)])
                    def fp(h, s, ch=ch, tgt=tgt):
                        return h.matmul(self.PR[0:1, 0:512],
                                        self.SCB[:, 32 + ch:33 + ch],
                                        self.WT[:, (ch % 4) * 512:(ch % 4 + 1) * 512],
                                        start=(ch == 0), stop=(ch == IC - 1))
                    self.pe(fp, dma_waits=[("dld", tgt)])
                tb = self.load("dlb", self.ROWF[0:1, 512:1024],
                               self.inp["l1_d2b"][0:1, osl],
                               ewaits=[("vector", sq.counts["vector"])])
                def fba(h, s):
                    return h.tensor_tensor(self.ROWF[0:1, 0:512],
                                           self.PR[0:1, 0:512],
                                           self.ROWF[0:1, 512:1024], ALU.add)
                self.ve(fba, dma_waits=[("dlb", tb)])
                pms = sq.dma_counts.get("msc", 0)
                t1 = self.store("rcw", self.s_rf, self.ROWF[0:1, 0:512],
                                dwaits=[("msc", pms)] if pms else ())
                tgu = self.load("msc", self.SC[:, 24:28],
                                self.s_rf.rearrange("a (c p) -> p c", p=128),
                                dwaits=[("rcw", t1)], slow=True)
                def fup(h, s, os2=os2, tgu=tgu):
                    return h.tensor_tensor(self.THC[:, os2 * 4:(os2 + 1) * 4],
                                           self.THC[:, os2 * 4:(os2 + 1) * 4],
                                           self.SC[:, 24:28], ALU.add)
                self.ve(fup, dma_waits=[("msc", tgu)])
        def fct(h, s):
            return h.tensor_copy(self.SCB[:, 0:16], self.THC[:])
        self.ve(fct)
        self._colproj(self.inp["l1_bc"], self.SCB[:, 0:16], self.BCC,
                      self.BCBC, "bcw")
        for j in range(TSB):
            tsl = slice(j * 512, (j + 1) * 512)
            def fnb(h, s, tsl=tsl):
                return h.matmul(self.P4[:, 0:512], self.CONES[0:1, :],
                                self.RB[0:1, tsl], start=True, stop=True)
            self.pe(fnb)
            def fnc(h, s):
                return h.activation(self.MSW[:, 0:512], self.P4[:, 0:512],
                                    AF.Copy)
            self.se(fnc)
            for ch in range(DC):
                def fz1(h, s, ch=ch, tsl=tsl):
                    return h.tensor_tensor(self.vf(0), self.xch(ch)[:, tsl],
                                           self.MSW[:, 0:512], ALU.add)
                self.ve(fz1)
                def fz2(h, s, tsl=tsl):
                    return h.tensor_tensor(self.vf(1), self.vf(0),
                                           self.RBC[:, tsl], ALU.mult)
                self.ve(fz2)
                def fgb(h, s, ch=ch, tsl=tsl):
                    return h.scalar_tensor_tensor(self.ach(ch)[:, tsl],
                                                  self.ach(ch)[:, tsl],
                                                  self.BCC[:, ch:ch + 1],
                                                  self.vf(1), ALU.mult,
                                                  ALU.add)
                self.ve(fgb)
        self.ph_stats(eps_col=1, src=self.ach)
        for j in range(TSB):
            tsl = slice(j * 512, (j + 1) * 512)
            def fnb(h, s, tsl=tsl):
                return h.matmul(self.P4[:, 0:512], self.CONES[0:1, :],
                                self.RB[0:1, tsl], start=True, stop=True)
            self.pe(fnb)
            def fnc(h, s):
                return h.activation(self.MSW[:, 0:512], self.P4[:, 0:512],
                                    AF.Copy)
            self.se(fnc)
            for ch in range(DC):
                def f1(h, s, ch=ch, tsl=tsl):
                    return h.tensor_tensor(self.vf(0), self.ach(ch)[:, tsl],
                                           self.MSW[:, 0:512], ALU.add)
                self.ve(f1)
                def f2(h, s, tsl=tsl):
                    return h.tensor_tensor(self.vf(1), self.vf(0),
                                           self.RBC[:, tsl], ALU.mult)
                self.ve(f2)
                def f3a(h, s, ch=ch):
                    return h.activation(self.vf(3), self.vf(1), AF.Copy,
                                        scale=gcol(4)[:, ch:ch + 1])
                self.se(f3a)
                prev_out = sq.dma_counts.get("out", 0)
                def f3(h, s, ch=ch, tsl=tsl):
                    return h.scalar_tensor_tensor(self.vf(2), self.vf(3),
                                                  gcol(5)[:, ch:ch + 1],
                                                  self.xch(ch)[:, tsl],
                                                  ALU.add, ALU.add)
                self.ve(f3, dma_waits=([("out", prev_out)] if prev_out else []))
                self.store("out", self.y_t[ch * 128:(ch + 1) * 128, tsl],
                           self.vf(2))

    def _vec_ln_cols(self, src, dstb, gcol_g, gcol_b, eps_col, dsum, dss):
        epsv = 1e-5 if eps_col == 0 else 1e-6
        def fa(h, s, dsum=dsum):
            return h.tensor_scalar(self.ROWF[0:1, 4:5], dsum, 1.0 / D, None,
                                   ALU.mult)
        self.ve(fa)
        def fb(h, s, dss=dss):
            return h.tensor_scalar(self.ROWF[0:1, 5:6], dss, 1.0 / D, None,
                                   ALU.mult)
        self.ve(fb)
        def fc(h, s):
            h.matmul(self.P4[:, 0:1], self.ONERF[0:1, :], self.ROWF[0:1, 4:5],
                     start=True, stop=True)
            return h.matmul(self.P4[:, 1:2], self.ONERF[0:1, :],
                            self.ROWF[0:1, 5:6], start=True, stop=True)
        self.pe(fc)
        def fd(h, s):
            return h.activation(self.SC[:, 20:22], self.P4[:, 0:2], AF.Copy)
        self.se(fd)
        def fe(h, s):
            return h.tensor_tensor(self.SC[:, 22:23], self.SC[:, 20:21],
                                   self.SC[:, 20:21], ALU.mult)
        self.ve(fe)
        def ff(h, s):
            return h.tensor_tensor(self.SC[:, 23:24], self.SC[:, 21:22],
                                   self.SC[:, 22:23], ALU.subtract)
        self.ve(ff)
        def fg(h, s, epsv=epsv):
            return h.tensor_scalar(self.SC[:, 22:23], self.SC[:, 23:24], epsv,
                                   None, ALU.add)
        self.ve(fg)
        def fh(h, s):
            return h.reciprocal(self.SC[:, 23:24], self.SC[:, 22:23])
        self.ve(fh)
        def fi(h, s):
            return h.activation(self.SC[:, 21:22], self.SC[:, 23:24], AF.Sqrt)
        self.se(fi)
        def f1(h, s, src=src):
            return h.scalar_tensor_tensor(self.SC[:, 48:64], src,
                                          self.SC[:, 20:21],
                                          self.Z512[:, 0:16], ALU.subtract,
                                          ALU.add)
        self.ve(f1)
        def f2(h, s):
            return h.scalar_tensor_tensor(self.SC[:, 24:40], self.SC[:, 48:64],
                                          self.SC[:, 21:22],
                                          self.Z512[:, 0:16], ALU.mult,
                                          ALU.add)
        self.ve(f2)
        def f3(h, s, gg=gcol_g):
            return h.tensor_tensor(self.SC[:, 48:64], self.SC[:, 24:40], gg,
                                   ALU.mult)
        self.ve(f3)
        def f4(h, s, gb=gcol_b, dstb=dstb):
            return h.tensor_tensor(dstb, self.SC[:, 48:64], gb, ALU.add)
        self.ve(f4)

    def _colproj(self, wsrc, xcol_b, dstcol, bias_col, fam):
        sq = self.seq
        for osl in range(4):
            csl = slice(osl * 512, (osl + 1) * 512)
            for ch in range(DC):
                tgt = self.load(fam, self.W32[:, 0:512],
                                wsrc[ch * 128:(ch + 1) * 128, csl],
                                ewaits=[("vector", sq.counts["vector"])])
                def fc(h, s, ch=ch):
                    return h.tensor_copy(self.WT[:, (ch % 4) * 512:(ch % 4 + 1) * 512],
                                         self.W32[:, 0:512])
                self.ve(fc, dma_waits=[(fam, tgt)])
                def fp(h, s, ch=ch, xcol_b=xcol_b):
                    return h.matmul(self.PR[0:1, 0:512], xcol_b[:, ch:ch + 1],
                                    self.WT[:, (ch % 4) * 512:(ch % 4 + 1) * 512],
                                    start=(ch == 0), stop=(ch == DC - 1))
                self.pe(fp)
            def fcr(h, s):
                return h.tensor_copy(self.ROWF[0:1, 0:512], self.PR[0:1, 0:512])
            self.ve(fcr)
            pms = sq.dma_counts.get("msc", 0)
            t1 = self.store("rcw", self.s_rf, self.ROWF[0:1, 0:512],
                            dwaits=[("msc", pms)] if pms else ())
            tgu = self.load("msc", self.SC[:, 24:28],
                            self.s_rf.rearrange("a (c p) -> p c", p=128),
                            dwaits=[("rcw", t1)], slow=True)
            def fadd(h, s, osl=osl, dstcol=dstcol, bias_col=bias_col):
                return h.tensor_tensor(dstcol[:, osl * 4:(osl + 1) * 4],
                                       self.SC[:, 24:28],
                                       bias_col[:, osl * 4:(osl + 1) * 4],
                                       ALU.add)
            self.ve(fadd, dma_waits=[("msc", tgu)])

    DIAG = False

    def ph_dump5(self):
        def fc1(h, s):
            h.tensor_copy(self.B[:, 0:16], self.THC[:])
            return h.tensor_copy(self.B[:, 16:32], self.BCC[:])
        self.ve(fc1)
        self.store("dbg", self.dbg[:, 2 * DC * N + 4096:2 * DC * N + 4192],
                   self.B[:, 0:96])

    def ph_dump(self):
        if self.DIAG:
            def fsc(h, s):
                return h.tensor_copy(self.B[:, 0:64], self.SC[:])
            self.ve(fsc)
            self.store("dbg", self.dbg[:, 2 * DC * N:2 * DC * N + 2048],
                       self.HB[:, 0:2048])
            self.store("dbg", self.dbg[:, 2 * DC * N + 2048:2 * DC * N + 4096],
                       self.WT[:])
            self.store("dbg", self.dbg[:, 2 * DC * N + 4096:2 * DC * N + 4160],
                       self.B[:, 0:64])
        for ch in range(DC):
            self.store("dbg", self.dbg[:, ch * N:(ch + 1) * N], self.xch(ch))
        for ch in range(DC):
            self.store("dbg", self.dbg[:, (DC + ch) * N:(DC + ch + 1) * N],
                       self.ach(ch))
        if not self.DIAG:
            self.store("dbg", self.dbg[:, 2 * DC * N:2 * DC * N + 4096],
                       self.C[:])

    def finish(self):
        fams = [f for f in self.seq.dma_counts if f in ("out", "dbg")]
        def fw(h, s, fams=tuple(fams)):
            r = None
            for f in fams:
                r = h.wait_ge(s.dsem[f], self.seq.dma_counts[f])
            return r
        self.seq.step("sync", fw, auto=True, inc=False)

    def build(self):
        p = self.phases
        self.ph_init()
        if p >= 1:
            self.ph_stats(eps_col=0)
            self.ph_l0_attn()
        if p >= 2:
            self.ph_prepass()
            self.ph_stats(eps_col=0)
            self.ph_l0_ffn_a()
        if p >= 3:
            self.ph_l0_ffn_b()
            self.ph_l0_ffn_c()
        if p >= 4:
            self.ph_stats(eps_col=0)
            self.ph_l1_attn()
        if p >= 5:
            self.ph_l1_ffn()
        if self.trunc:
            if p >= 5:
                self.ph_dump5()
            self.ph_dump()
        self.finish()
        self.emit()
        return self.nc

    def emit(self):
        nc, seq = self.nc, self.seq
        es = self.es
        class S:
            pass
        sems = S()
        sems.dsem = {}
        for f in seq.dma_counts:
            sems.dsem[f] = es.enter_context(nc.semaphore("d_" + f))
        esem = {e: es.enter_context(nc.semaphore("e_" + e))
                for e in ENGS if e != "sync"}
        block = es.enter_context(nc.Block())
        per_eng = {e: [] for e in ENGS}
        for it in seq.items:
            per_eng[it[0]].append(it)

        _OPS = ("tensor_tensor", "tensor_scalar", "scalar_tensor_tensor",
                "tensor_copy", "memset", "reduce_sum", "reduce_max",
                "reciprocal", "activation", "copy", "tensor_reduce",
                "stream_shuffle")

        class DrainWrap:
            def __init__(self, h):
                self._h = h

            def __getattr__(self, nm):
                f = getattr(self._h, nm)
                if nm not in _OPS:
                    return f
                def g(*a, **k):
                    r = f(*a, **k)
                    self._h.drain()
                    return r
                return g

        def run(handle, eng):
            if eng in ("vector", "scalar"):
                handle = DrainWrap(handle)
            for (e, waits, dw, fn, inc) in per_eng[eng]:
                for (o, cnt) in waits:
                    handle.wait_ge(esem[o], cnt)
                for (f, cnt) in dw:
                    handle.wait_ge(sems.dsem[f], cnt)
                r = fn(handle, sems)
                if inc:
                    if r is None:
                        raise RuntimeError(f"step on {eng} returned None")
                    r.then_inc(esem[eng], 1)

        @block.tensor
        def _(h):
            run(h, "tensor")

        @block.vector
        def _(h):
            run(h, "vector")

        @block.scalar
        def _(h):
            run(h, "scalar")

        @block.sync
        def _(h):
            run(h, "sync")

        es.close()


def _consts():
    ii = np.arange(128)
    had = np.where((ii[:, None] & ii[None, :] & 0x7F).astype(np.int64) >= 0, 0, 0)
    pop = np.zeros((128, 128), dtype=np.int64)
    a = ii[:, None] & ii[None, :]
    for b in range(7):
        pop += (a >> b) & 1
    had = np.where(pop % 2 == 0, 1.0, -1.0).astype(BF)
    iden = np.eye(128, dtype=BF)
    ones = np.ones((128, 128), dtype=BF)
    kk = np.arange(128)[:, None]
    cc = np.arange(896)[None, :]
    mk = (kk <= cc - 384).astype(np.float32).astype(BF)
    pos = np.arange(N, dtype=np.float32).reshape(1, N)
    j64 = (np.arange(128) % 64).astype(np.float32).reshape(128, 1)
    neg1 = np.full((1, 512), -1.0, dtype=np.float32)
    onesf = np.ones((128, 1), dtype=np.float32)
    onerf = np.ones((1, 128), dtype=np.float32)
    return dict(c_had=np.ascontiguousarray(had), c_iden=iden, c_ones=ones,
                c_mk=np.ascontiguousarray(mk), c_pos=pos, c_j64=j64,
                c_neg1=neg1, c_onesf=onesf, c_onerf=onerf)


def _col16(v):
    return np.ascontiguousarray(v.reshape(DC, 128).T.astype(np.float32))


_CACHE = {}


def kernel(L0_attn_x, L0_attn_Wq, L0_attn_Wk, L0_attn_Wv, L0_attn_Wo,
           L0_ffn_w_gate, L0_ffn_w_up, L0_ffn_w_down,
           L1_attn_Wq, L1_attn_Wk, L1_attn_Wv, L1_attn_Wo,
           L1_ffn_in_gamma, L1_ffn_in_beta, L1_ffn_agg_w, L1_ffn_agg_b,
           L1_ffn_pn_gamma, L1_ffn_pn_beta, L1_ffn_d1_w, L1_ffn_d1_b,
           L1_ffn_d2_w, L1_ffn_d2_b, L1_ffn_bc_w, L1_ffn_bc_b,
           L1_ffn_gate_w, L1_ffn_gate_b, L1_ffn_out_gamma, L1_ffn_out_beta):
    asc = np.ascontiguousarray
    f32 = lambda v: asc(np.asarray(v, dtype=np.float32))
    gam = np.stack([_col16(f32(v)) for v in
                    (L1_ffn_in_gamma, L1_ffn_in_beta, L1_ffn_pn_gamma,
                     L1_ffn_pn_beta, L1_ffn_out_gamma, L1_ffn_out_beta,
                     L1_ffn_gate_b)], axis=0)
    gam = asc(np.concatenate([g for g in gam], axis=1))
    shared = dict(
        l0_wq=asc(f32(L0_attn_Wq).T), l0_wk=asc(f32(L0_attn_Wk).T),
        l0_wv=asc(f32(L0_attn_Wv).T), l0_wo=asc(f32(L0_attn_Wo).T),
        l0_wg=asc(f32(L0_ffn_w_gate).T), l0_wu=asc(f32(L0_ffn_w_up).T),
        l0_wd=asc(f32(L0_ffn_w_down).T),
        l1_wq=asc(f32(L1_attn_Wq).T), l1_wk=asc(f32(L1_attn_Wk).T),
        l1_wv=asc(f32(L1_attn_Wv).T), l1_wo=asc(f32(L1_attn_Wo).T),
        l1_agg=f32(L1_ffn_agg_w), l1_bc=f32(L1_ffn_bc_w),
        l1_gw=f32(L1_ffn_gate_w),
        l1_d1=f32(L1_ffn_d1_w), l1_d2=f32(L1_ffn_d2_w),
        l1_d1b=f32(L1_ffn_d1_b).reshape(1, I8),
        l1_d2b=f32(L1_ffn_d2_b).reshape(1, D),
        gam=gam, aggb_c=_col16(f32(L1_ffn_agg_b)),
        bcb_c=_col16(f32(L1_ffn_bc_b)),
        **_consts())
    x = f32(L0_attn_x)
    key = ("prog", PHASES)
    if key not in _CACHE:
        _CACHE[key] = Prog(PHASES).build()
    nc = _CACHE[key]
    in_maps = []
    for b in range(8):
        m = dict(shared)
        m["x_t"] = asc(x[b].T)
        in_maps.append(m)
    res = run_bass_kernel_spmd(nc, in_maps, core_ids=list(range(8)))
    out = np.stack([np.asarray(res.results[b]["y_t"]).T for b in range(8)],
                   axis=0)
    return np.ascontiguousarray(out.astype(np.float32))
